# version 05
# baseline (speedup 1.0000x reference)
import numpy as np
import ml_dtypes
from contextlib import ExitStack

import concourse.bass as bass
import concourse.mybir as mybir
from concourse.bass_utils import run_bass_kernel_spmd

F32 = mybir.dt.float32
BF16 = mybir.dt.bfloat16
AF = mybir.ActivationFunctionType
ALU = mybir.AluOpType
AX = mybir.AxisListType

S = 4096
CD = 4096
NI = 7168
INTER = 448
TOK = 2048
SCALE = 1.0 / float(np.sqrt(128.0))
EXP_SHIFT = -12.0
EPS = 1e-5

STAGE = "full"

SEMS = ["sLd", "sSt", "sSc", "sGc", "sPe", "sAc", "sVe", "sGp"]

QT0 = 0
WQT0 = 32768
XTT0 = 49152
KT0 = 65536
VA0 = 69632
WKT0 = 73760
WVT0 = 77856
G1_COLS = 81952
PRB0 = WKT0


class E:

    def __init__(self, nc, active, eng, sems):
        self.nc = nc
        self.a = active
        self.e = eng
        self.sems = sems
        self.c = {k: 0 for k in sems}

    def dma(self, q, out, in_, sem, tr=False):
        self.c[sem] += 16
        if self.a == q:
            if tr:
                self.e.dma_start_transpose(out, in_).then_inc(self.sems[sem], 16)
            else:
                self.e.dma_start(out=out, in_=in_).then_inc(self.sems[sem], 16)
        return self.c[sem]

    def wait(self, engname, sem, val):
        if val > self.c[sem]:
            raise RuntimeError(f"wait {sem}>={val} but max {self.c[sem]}")
        if val > 0 and self.a == engname:
            self.e.wait_ge(self.sems[sem], val)

    def waitt(self, engname, tup):
        if tup is not None and tup[0] is not None:
            self.wait(engname, tup[0], tup[1])

    def _fin(self, ins, sem):
        if sem is not None:
            self.c[sem] += 1
            if ins is not None:
                ins.then_inc(self.sems[sem], 1)
        return self.c[sem] if sem else None

    def mm(self, out, lhsT, rhs, start, stop, sem=None):
        ins = None
        if self.a == "tensor":
            ins = self.e.matmul(out, lhsT, rhs, start=start, stop=stop)
        return self._fin(ins, sem)

    def tr(self, out, in_, ident, sem=None):
        ins = None
        if self.a == "tensor":
            ins = self.e.transpose(out, in_, ident)
        return self._fin(ins, sem)

    def act(self, out, in_, func, bias=0.0, scale=1.0, accum=None, sem=None):
        ins = None
        if self.a == "scalar":
            ins = self.e.activation(out, in_, func, bias=bias, scale=scale,
                                    accum_out=accum)
        return self._fin(ins, sem)

    def vop(self, name, *args, sem=None, eng="vector", **kw):
        ins = None
        if self.a == eng:
            ins = getattr(self.e, name)(*args, **kw)
        return self._fin(ins, sem)


def build_nc(stage=None):
    stage = stage or STAGE
    nc = bass.Bass("TRN2", target_bir_lowering=False, debug=False)

    x_b = nc.dram_tensor("x_b", [S, CD], F32, kind="ExternalInput").ap()
    xr_b = nc.dram_tensor("xr_b", [TOK, CD], F32, kind="ExternalInput").ap()
    wq = nc.dram_tensor("wq", [2048, CD], F32, kind="ExternalInput").ap()
    bq2 = nc.dram_tensor("bq2", [128, 16], F32, kind="ExternalInput").ap()
    wk = nc.dram_tensor("wk", [128, CD], F32, kind="ExternalInput").ap()
    bk2 = nc.dram_tensor("bk2", [128, 1], F32, kind="ExternalInput").ap()
    wv = nc.dram_tensor("wv", [128, CD], F32, kind="ExternalInput").ap()
    bv2 = nc.dram_tensor("bv2", [128, 1], F32, kind="ExternalInput").ap()
    wo = nc.dram_tensor("wo", [CD, CD], F32, kind="ExternalInput").ap()
    bo_bc = nc.dram_tensor("bo_bc", [128, CD], F32, kind="ExternalInput").ap()
    gwi = nc.dram_tensor("gwi", [16, CD], F32, kind="ExternalInput").ap()
    gmat = nc.dram_tensor("gmat", [NI, CD], F32, kind="ExternalInput").ap()
    umat = nc.dram_tensor("umat", [NI, CD], F32, kind="ExternalInput").ap()
    dmat = nc.dram_tensor("dmat", [CD, NI], F32, kind="ExternalInput").ap()
    consts = nc.dram_tensor("consts", [128, 8], F32, kind="ExternalInput").ap()
    eye_bf_d = nc.dram_tensor("eye_bf", [128, 128], BF16, kind="ExternalInput").ap()
    eye_f_d = nc.dram_tensor("eye_f", [128, 16], F32, kind="ExternalInput").ap()

    out = nc.dram_tensor("out", [TOK, CD], F32, kind="ExternalOutput").ap()
    dbg = {}
    if stage != "full":
        dbg["qt"] = nc.dram_tensor("dbg_qt", [128, 32768], BF16, kind="ExternalOutput").ap()
        dbg["kt"] = nc.dram_tensor("dbg_kt", [128, 4096], BF16, kind="ExternalOutput").ap()
        dbg["va"] = nc.dram_tensor("dbg_va", [128, 4128], BF16, kind="ExternalOutput").ap()
        dbg["x1f"] = nc.dram_tensor("dbg_x1f", [TOK, CD], F32, kind="ExternalOutput").ap()
        dbg["hw"] = nc.dram_tensor("dbg_hw", [TOK, NI], BF16, kind="ExternalOutput").ap()
        dbg["w"] = nc.dram_tensor("dbg_w", [128, 256], F32, kind="ExternalOutput").ap()

    x_bf = nc.dram_tensor("x_bf", [S, CD], BF16, kind="Internal").ap()
    wq_bf = nc.dram_tensor("wq_bf", [2048, CD], BF16, kind="Internal").ap()
    wo_bf = nc.dram_tensor("wo_bf", [CD, CD], BF16, kind="Internal").ap()
    g_bf = nc.dram_tensor("g_bf", [NI, CD], BF16, kind="Internal").ap()
    u_bf = nc.dram_tensor("u_bf", [NI, CD], BF16, kind="Internal").ap()
    d_bf = nc.dram_tensor("d_bf", [CD, NI], BF16, kind="Internal").ap()
    x1pre = nc.dram_tensor("x1pre", [TOK, CD], F32, kind="Internal").ap()
    x1bf_d = nc.dram_tensor("x1bf_d", [TOK, CD], BF16, kind="Internal").ap()
    x1f_d = nc.dram_tensor("x1f_d", [TOK, CD], F32, kind="Internal").ap()
    hw_d = nc.dram_tensor("hw_d", [TOK, NI], BF16, kind="Internal").ap()
    cp0 = nc.dram_tensor("cp0", [TOK, 3072], F32, kind="Internal").ap()

    XTS = nc.dram_tensor("XTS", [128, 8 * 16384], BF16, kind="Internal").ap()
    WQS = nc.dram_tensor("WQS", [128, 4 * 16384], BF16, kind="Internal").ap()
    WOS = nc.dram_tensor("WOS", [128, 8 * 16384], BF16, kind="Internal").ap()
    GUSg = nc.dram_tensor("GUSg", [128, 14 * 16384], BF16, kind="Internal").ap()
    GUSu = nc.dram_tensor("GUSu", [128, 14 * 16384], BF16, kind="Internal").ap()
    DTS = nc.dram_tensor("DTS", [128, 4 * 56 * 1024], BF16, kind="Internal").ap()
    HWS = nc.dram_tensor("HWS", [128, 56 * 2048], BF16, kind="Internal").ap()
    X1TS = nc.dram_tensor("X1TS", [128, 2 * 32768], BF16, kind="Internal").ap()

    G1 = nc.alloc_sbuf_tensor("G1", [128, G1_COLS], BF16)
    TRB = nc.alloc_sbuf_tensor("TRB", [128, 4096], BF16)
    sm_f = nc.alloc_sbuf_tensor("sm_f", [128, 2048], F32)
    sm2_f = nc.alloc_sbuf_tensor("sm2_f", [128, 2048], F32)
    st_f = nc.alloc_sbuf_tensor("st_f", [128, 272], F32)
    lg_f = nc.alloc_sbuf_tensor("lg_f", [128, 256], F32)
    w_f = nc.alloc_sbuf_tensor("w_f", [128, 256], F32)
    tk_f = nc.alloc_sbuf_tensor("tk_f", [128, 64], F32)
    gwt_sb = nc.alloc_sbuf_tensor("gwt_sb", [128, 512], BF16)
    bqs = nc.alloc_sbuf_tensor("bqs", [128, 16], F32)
    bks = nc.alloc_sbuf_tensor("bks", [128, 1], F32)
    bvs = nc.alloc_sbuf_tensor("bvs", [128, 1], F32)
    cst = nc.alloc_sbuf_tensor("cst", [128, 8], F32)
    eyeb = nc.alloc_sbuf_tensor("eyeb", [128, 128], BF16)
    eyef = nc.alloc_sbuf_tensor("eyef", [128, 16], F32)
    bos = nc.alloc_sbuf_tensor("bos", [128, CD], F32)
    PS = nc.alloc_psum_tensor("PS", [128, 4096], F32)

    G = G1[:, :]
    Gf = G1[:, :].bitcast(F32)
    PSf = PS[:, :]
    PSb = PS[:, :].bitcast(BF16)
    smb = sm_f[:, :].bitcast(BF16)
    sm2b = sm2_f[:, :].bitcast(BF16)
    lgb = lg_f[:, :].bitcast(BF16)
    wfb = w_f[:, :].bitcast(BF16)

    with ExitStack() as ctx:
        sems = {n: ctx.enter_context(nc.semaphore(n)) for n in SEMS}
        block = ctx.enter_context(nc.Block())

        def emit(active, eng):
            e = E(nc, active, eng, sems)

            e.dma("sync", cst[:, :], consts, "sLd")
            e.dma("sync", eyeb[:, :], eye_bf_d, "sLd")
            e.dma("sync", eyef[:, :], eye_f_d, "sLd")
            e.dma("sync", bqs[:, :], bq2, "sLd")
            e.dma("sync", bks[:, :], bk2, "sLd")
            e.dma("sync", bvs[:, :], bv2, "sLd")
            e.dma("sync", bos[:, :], bo_bc, "sLd")
            c_consts = e.c["sLd"]

            e.dma("gpsimd", G[:, XTT0:XTT0+4096], wk, "sGc")
            e.dma("gpsimd", G[:, XTT0+4096:XTT0+8192], wv, "sGc")
            kv_cast = e.dma("gpsimd", G[0:16, XTT0+8192:XTT0+12288], gwi, "sGc")
            x_cast = []
            for i in range(8):
                x_cast.append(e.dma("gpsimd", x_bf[i*256:(i+1)*256, :],
                                    x_b[i*256:(i+1)*256, :], "sGc"))
            wq_cast = []
            for i in range(8):
                wq_cast.append(e.dma("gpsimd", wq_bf[i*256:(i+1)*256, :],
                                     wq[i*256:(i+1)*256, :], "sGc"))
            for i in range(8, 16):
                x_cast.append(e.dma("gpsimd", x_bf[i*256:(i+1)*256, :],
                                    x_b[i*256:(i+1)*256, :], "sGc"))
            e.wait("vector", "sLd", c_consts)
            e.vop("memset", G[:, VA0:VA0+4128], 1.0, sem="sVe")
            va_ones = e.c["sVe"]

            bounce4 = [("t0", TRB[:, 0:2048]), ("t1", TRB[:, 2048:4096]),
                       ("s0", smb[:, 0:2048]), ("s1", sm2b[:, 0:2048])]
            bounce2 = bounce4[:2]
            trb_state = {"bufs": bounce4, "free": {}, "i": 0}

            def pretr(src, dst, cast_wait, store_q="scalar"):
                bufs = trb_state["bufs"]
                key, buf = bufs[trb_state["i"] % len(bufs)]
                trb_state["i"] += 1
                R = src.shape[0]
                store_sem = "sSc" if store_q == "scalar" else "sGc"
                if isinstance(cast_wait, tuple):
                    e.wait("sync", cast_wait[0], cast_wait[1])
                else:
                    e.wait("sync", "sGc", cast_wait)
                fs = trb_state["free"].get(key)
                if fs is not None:
                    e.wait("sync", fs[0], fs[1])
                e.dma("sync", buf[:, 0:R], src, "sLd", tr=True)
                tr_done = e.c["sLd"]
                e.wait(store_q, "sLd", tr_done)
                cnt = e.dma(store_q, dst, buf[:, 0:R], store_sem)
                trb_state["free"][key] = (store_sem, cnt)
                return (store_sem, cnt)

            XTSr = XTS.rearrange("p (ss f) -> p ss f", f=16384)
            xts_done = [None, None]
            def emit_xts(rh):
                for cb in range(32):
                    dst = XTSr[:, rh*4:(rh+1)*4, :].rearrange(
                        "p ss (cb f) -> p ss cb f", f=512)[:, :, cb, :]
                    xts_done[rh] = pretr(
                        x_bf[rh*2048:(rh+1)*2048, cb*128:(cb+1)*128],
                        dst, x_cast[rh*8+7])
            emit_xts(0)

            WQSr = WQS.rearrange("p (jq f) -> p jq f", f=16384)
            wqs_done = None
            for cb in range(32):
                dst = WQSr.rearrange("p jq (cb f) -> p jq cb f", f=512)[:, :, cb, :]
                wqs_done = pretr(wq_bf[0:2048, cb*128:(cb+1)*128], dst,
                                 wq_cast[7])
            emit_xts(1)

            e.waitt("gpsimd", xts_done[1])
            wo_cast = []
            for i in range(16):
                wo_cast.append(e.dma("gpsimd", wo_bf[i*256:(i+1)*256, :],
                                     wo[i*256:(i+1)*256, :], "sGc"))
            g_cast = []
            for i in range(14):
                g_cast.append(e.dma("gpsimd", g_bf[i*512:(i+1)*512, :],
                                    gmat[i*512:(i+1)*512, :], "sGc"))
            u_cast = []
            for i in range(14):
                u_cast.append(e.dma("gpsimd", u_bf[i*512:(i+1)*512, :],
                                    umat[i*512:(i+1)*512, :], "sGc"))
            d_cast = []
            for i in range(16):
                d_cast.append(e.dma("gpsimd", d_bf[i*256:(i+1)*256, :],
                                    dmat[i*256:(i+1)*256, :], "sGc"))

            xtt_half_free = [0, 0]
            wqt_free = 0
            qbank_free = [0, 0, 0, 0]
            kvbank_done = [0, 0]
            vx_free = [0, 0]
            wo_strip_free = [0, 0]
            sweep_done_v = 0
            sweep_done_st = 0
            x1p_store = {}
            x1bf_stores = 0
            x1bf_pass_store = [0, 0]
            x1f_stores = 0
            smf_ve = [0] * 4
            smf_st = [0] * 4
            smf_ac = [0] * 4
            post_attn_ac = 0
            last_av_free = 0
            wos_done = None
            gus_done = None
            dts_done = []
            x1ts_done = [None, None]
            tb_free = [0, 0]

            e.wait("tensor", "sGc", kv_cast)
            e.wait("tensor", "sLd", c_consts)
            for i in range(2):
                src0 = XTT0 + i * 4096
                dst0 = WKT0 + i * 4096
                for cb in range(32):
                    b = cb % 2
                    pb0 = (6 + b) * 1024
                    e.wait("tensor", "sVe", tb_free[b])
                    e.tr(PSb[:, pb0:pb0+128], G[:, src0+cb*128:src0+(cb+1)*128],
                         eyeb[:, :], sem="sPe")
                    e.wait("vector", "sPe", e.c["sPe"])
                    tb_free[b] = e.vop("tensor_copy",
                                       G[:, dst0+cb*128:dst0+(cb+1)*128],
                                       PSb[:, pb0:pb0+128], sem="sVe")
            for cb in range(32):
                b = cb % 2
                pb0 = (6 + b) * 1024
                e.wait("tensor", "sVe", tb_free[b])
                e.tr(PSb[:, pb0:pb0+16],
                     G[0:16, XTT0+8192+cb*128:XTT0+8192+(cb+1)*128],
                     eyeb[0:16, 0:16], sem="sPe")
                e.wait("vector", "sPe", e.c["sPe"])
                tb_free[b] = e.vop("tensor_copy",
                                   gwt_sb[:, cb*16:(cb+1)*16],
                                   PSb[:, pb0:pb0+16], sem="sVe")
            wkvt_done_v = e.c["sVe"]
            wkvt_done_p = e.c["sPe"]

            for p in range(2):
                for jq in range(2):
                    e.waitt("sync", wqs_done)
                    e.wait("sync", "sPe", wqt_free)
                    e.dma("sync", G[:, WQT0:WQT0+16384],
                          WQS[:, (p*2+jq)*16384:(p*2+jq+1)*16384], "sLd")
                    wq_ready = e.c["sLd"]
                    for ss in range(8):
                        xb0 = XTT0
                        rh = ss // 4
                        e.waitt("sync", xts_done[rh])
                        xtt_ready = [0, 0]
                        for hf in range(2):
                            e.wait("sync", "sPe", xtt_half_free[hf])
                            e.dma("sync", G[:, xb0+hf*8192:xb0+(hf+1)*8192],
                                  XTS[:, ss*16384+hf*8192:ss*16384+(hf+1)*8192],
                                  "sLd")
                            xtt_ready[hf] = e.c["sLd"]
                        if p == 0 and jq == 0 and ss == 0:
                            e.wait("sync", "sPe", wkvt_done_p)

                        def xtt(cb):
                            return G[:, xb0+cb*512:xb0+(cb+1)*512]

                        kv = (p == 0 and jq == 0)
                        e.wait("tensor", "sLd", max(xtt_ready[0], wq_ready))
                        for sl in range(4):
                            e.wait("tensor", "sVe", qbank_free[sl])
                        if kv:
                            e.wait("tensor", "sVe", max(kvbank_done))
                        for cb in range(32):
                            if cb == 16:
                                e.wait("tensor", "sLd", xtt_ready[1])
                            if kv:
                                e.mm(PSf[:, 2048:2560],
                                     G[:, WKT0+cb*128:WKT0+(cb+1)*128],
                                     xtt(cb), cb == 0, cb == 31,
                                     sem="sPe" if cb == 31 else None)
                                e.mm(PSf[:, 2560:3072],
                                     G[:, WVT0+cb*128:WVT0+(cb+1)*128],
                                     xtt(cb), cb == 0, cb == 31,
                                     sem="sPe" if cb == 31 else None)
                            for sl in range(4):
                                e.mm(PSf[:, sl*512:(sl+1)*512],
                                     G[:, WQT0+cb*512+sl*128:WQT0+cb*512+(sl+1)*128],
                                     xtt(cb), cb == 0, cb == 31,
                                     sem="sPe" if sl == 3 else None)
                                if sl == 3 and (cb == 15 or cb == 31):
                                    xtt_half_free[cb // 16] = e.c["sPe"]
                        qmm_done = e.c["sPe"]
                        if kv:
                            e.wait("vector", "sPe", qmm_done)
                            e.vop("tensor_scalar",
                                  G[:, KT0+ss*512:KT0+(ss+1)*512],
                                  PSf[:, 2048:2560], bks[:, 0:1], None,
                                  ALU.add, sem="sVe")
                            kvbank_done[0] = e.c["sVe"]
                            vtb = lgb if ss % 2 == 0 else wfb
                            e.vop("tensor_scalar", vtb[:, 0:512],
                                  PSf[:, 2560:3072], bvs[:, 0:1], None,
                                  ALU.add, sem="sVe")
                            kvbank_done[1] = e.c["sVe"]
                            vt_evac = e.c["sVe"]
                            for st in range(4):
                                b = st % 2
                                pb0 = (6 + b) * 1024
                                kt = ss * 4 + st
                                e.wait("tensor", "sVe", vt_evac)
                                e.wait("tensor", "sVe", vx_free[b])
                                e.wait("tensor", "sVe", va_ones)
                                e.tr(PSb[:, pb0:pb0+128],
                                     vtb[:, st*128:(st+1)*128],
                                     eyeb[:, :], sem="sPe")
                                e.wait("vector", "sPe", e.c["sPe"])
                                vx_free[b] = e.vop(
                                    "tensor_copy",
                                    G[:, VA0+kt*129:VA0+kt*129+128],
                                    PSb[:, pb0:pb0+128], sem="sVe")
                        for sl in range(4):
                            slot = jq * 4 + sl
                            e.wait("vector", "sPe", qmm_done)
                            qbank_free[sl] = e.vop(
                                "tensor_scalar",
                                G[:, QT0+slot*4096+ss*512:QT0+slot*4096+(ss+1)*512],
                                PSf[:, sl*512:(sl+1)*512],
                                bqs[:, p*8+slot:p*8+slot+1], None,
                                ALU.add, sem="sVe")
                    wqt_free = e.c["sPe"]
                qkv_done_v = e.c["sVe"]

                if p == 0:
                    WOSr = WOS.rearrange("p (jc f) -> p jc f", f=16384)
                    for cb2 in range(64):
                        qb, half = cb2 // 2, cb2 % 2
                        dst = WOSr[:, half*4:(half+1)*4, :].rearrange(
                            "p jc (qb f) -> p jc qb f", f=512)[:, :, qb, :]
                        wos_done = pretr(
                            wo_bf[half*2048:(half+1)*2048, qb*128:(qb+1)*128],
                            dst, wo_cast[15], store_q="gpsimd")

                wo_strip_free[0] = max(wo_strip_free[0], wqt_free)
                wo_strip_free[1] = max(wo_strip_free[1], max(xtt_half_free))
                wot_ready = {}
                for jc in range(2):
                    wb0 = WQT0 if jc % 2 == 0 else XTT0
                    e.waitt("sync", wos_done)
                    e.wait("sync", "sPe", wo_strip_free[jc % 2])
                    e.dma("sync", G[:, wb0:wb0+16384],
                          WOS[:, jc*16384:(jc+1)*16384], "sLd")
                    wot_ready[jc] = e.c["sLd"]

                if stage == "qkv" and p == 0:
                    e.wait("sync", "sVe", qkv_done_v)
                    e.dma("sync", dbg["qt"], G[:, 0:32768], "sSt")
                    e.dma("sync", dbg["kt"], G[:, KT0:KT0+4096], "sSt")
                    e.dma("sync", dbg["va"], G[:, VA0:VA0+4128], "sSt")
                    e.wait("sync", "sSt", e.c["sSt"])
                    return

                exp_free = [0, 0]
                prb_free = [0, 0]
                av_free = 0
                sc_done = {}
                e.wait("tensor", "sVe", qkv_done_v)
                if p == 0:
                    e.wait("scalar", "sPe", wqt_free)
                else:
                    e.wait("scalar", "sVe", sweep_done_v)
                    e.wait("scalar", "sGc", sweep_done_st)

                def emit_scT(hh, qc, ktp):
                    buf = ktp % 2
                    sb0 = buf * 1024
                    q0 = QT0 + hh * 4096
                    e.wait("tensor", "sAc", exp_free[buf])
                    for half in range(2):
                        kt = ktp * 2 + half
                        e.mm(PSf[:, sb0+half*512:sb0+(half+1)*512],
                             G[:, KT0+kt*128:KT0+(kt+1)*128],
                             G[:, q0+qc*512:q0+(qc+1)*512],
                             True, True, sem="sPe" if half == 1 else None)
                    sc_done[(hh, qc, ktp)] = e.c["sPe"]

                triples = [(hh, qc, ktp) for hh in range(8)
                           for qc in range(8) for ktp in range(16)]
                for i, (hh, qc, ktp) in enumerate(triples):
                    buf = ktp % 2
                    sb0 = buf * 1024
                    q0 = QT0 + hh * 4096
                    if i == 0:
                        emit_scT(hh, qc, ktp)
                    e.wait("scalar", "sPe", sc_done[(hh, qc, ktp)])
                    e.wait("scalar", "sPe", prb_free[buf])
                    exp_free[buf] = e.act(
                        G[:, PRB0+sb0:PRB0+sb0+1024],
                        PSf[:, sb0:sb0+1024], AF.Exp,
                        bias=cst[:, 0:1], scale=SCALE, sem="sAc")
                    if i + 1 < len(triples):
                        emit_scT(*triples[i+1])
                    e.wait("tensor", "sAc", exp_free[buf])
                    if ktp == 0:
                        e.wait("tensor", "sVe", av_free)
                    for half in range(2):
                        kt = ktp * 2 + half
                        for qb in range(4):
                            a0 = 2048 + qb * 512
                            e.mm(PSf[:, a0:a0+129],
                                 G[:, PRB0+sb0+half*512+qb*128:
                                      PRB0+sb0+half*512+(qb+1)*128],
                                 G[:, VA0+kt*129:VA0+kt*129+129],
                                 kt == 0, kt == 31,
                                 sem="sPe" if (half == 1 and qb == 3) else None)
                    prb_free[buf] = e.c["sPe"]
                    if ktp == 15:
                        av_mm = e.c["sPe"]
                        e.wait("vector", "sPe", av_mm)
                        for qb in range(4):
                            a0 = 2048 + qb * 512
                            e.vop("reciprocal", st_f[:, 232+qb:233+qb],
                                  PSf[:, a0+128:a0+129],
                                  sem="sVe" if qb == 3 else None)
                        e.wait("vector", "sVe", e.c["sVe"])
                        for qb in range(4):
                            a0 = 2048 + qb * 512
                            qg = qc * 4 + qb
                            av_free = e.vop(
                                "tensor_scalar",
                                G[:, q0+qg*128:q0+(qg+1)*128],
                                PSf[:, a0:a0+128],
                                st_f[:, 232+qb:233+qb], None,
                                ALU.mult, sem="sVe")
                attn_done_v = e.c["sVe"]
                post_attn_ac = max(exp_free)
                last_av_free = av_free

                GUSgr = GUSg.rearrange("p (ic f) -> p ic f", f=16384)
                GUSur = GUSu.rearrange("p (ic f) -> p ic f", f=16384)
                X1Tr = X1TS.rearrange("p (th f) -> p th f", f=32768)

                def emit_gus(src_m, dstr, cw):
                    done = None
                    for ch in range(4):
                        r0 = ch * 2048
                        rr_n = min(2048, NI - r0)
                        nic = rr_n // 512
                        for cb in range(32):
                            dst = dstr[:, ch*4:ch*4+nic, :].rearrange(
                                "p ic (cb f) -> p ic cb f", f=512)[:, :, cb, :]
                            done = pretr(src_m[r0:r0+rr_n, cb*128:(cb+1)*128],
                                         dst, cw, store_q="gpsimd")
                    return done

                if p == 0:
                    gus_done_g = emit_gus(g_bf, GUSgr, g_cast[-1])
                    DTSr = DTS.rearrange("p (q f) -> p q f", f=56 * 1024)
                    for half in range(2):
                        for ib in range(56):
                            dst = DTSr.rearrange(
                                "p q (ib f) -> p q ib f", f=1024)[:, :, ib, :]
                            dts_done.append(pretr(
                                d_bf[half*2048:(half+1)*2048,
                                     ib*128:(ib+1)*128],
                                dst[:, half*2:half*2+2, :],
                                d_cast[7 if half == 0 else 15],
                                store_q="gpsimd"))
                else:
                    e.wait("sync", "sSt", smf_st[0])
                    e.wait("sync", "sSt", smf_st[1])
                    e.wait("sync", "sAc", smf_ac[0])
                    e.wait("sync", "sAc", smf_ac[1])
                    e.wait("sync", "sVe", smf_ve[0])
                    e.wait("sync", "sVe", smf_ve[1])
                    trb_state["bufs"] = bounce4
                    gus_done = emit_gus(u_bf, GUSur, u_cast[-1])
                    x1src = x1bf_pass_store[0]
                    for cb in range(32):
                        dst = X1Tr.rearrange(
                            "p th (cb f) -> p th cb f", f=1024)[:, 0, cb, :]
                        x1ts_done[0] = pretr(
                            x1bf_d[0:1024, cb*128:(cb+1)*128],
                            dst, x1src, store_q="gpsimd")

                if stage == "attn" and p == 0:
                    e.wait("sync", "sVe", attn_done_v)
                    e.dma("sync", dbg["qt"], G[:, 0:32768], "sSt")
                    e.wait("sync", "sSt", e.c["sSt"])
                    return

                wo_psum_free = [attn_done_v, attn_done_v]
                e.wait("tensor", "sAc", post_attn_ac)
                nrm_done = [0, 0]
                prb_f0 = WKT0 // 2
                x1bf_tile0 = WVT0
                wo_pe_start = e.c["sPe"]
                for bk in ("s0", "s1"):
                    bs = trb_state["free"].get(bk)
                    if bs is not None:
                        e.waitt("sync", bs)
                        e.waitt("vector", bs)
                        e.waitt("scalar", bs)
                trb_state["bufs"] = bounce2
                for jc in range(8):
                    wbuf = jc % 2
                    wb0 = WQT0 if wbuf == 0 else XTT0
                    for hh in range(8):
                        bank = hh % 2
                        q0 = QT0 + hh * 4096
                        e.wait("tensor", "sLd", wot_ready[jc])
                        e.wait("tensor", "sVe", wo_psum_free[bank])
                        for qb in range(32):
                            e.mm(PSf[:, bank*512:(bank+1)*512],
                                 G[:, q0+qb*128:q0+(qb+1)*128],
                                 G[:, wb0+qb*512:wb0+(qb+1)*512],
                                 qb == 0, qb == 31,
                                 sem="sPe" if qb == 31 else None)
                        womm = e.c["sPe"]
                        if hh == 7:
                            wo_strip_free[wbuf] = womm
                        ti = (jc * 8 + hh) % 4
                        rl0 = (p * 8 + hh) * 128
                        e.wait("sync", "sAc", smf_ac[ti])
                        e.wait("sync", "sVe", smf_ve[ti])
                        e.dma("sync", sm2_f[:, ti*512:(ti+1)*512],
                              xr_b[rl0:rl0+128, jc*512:(jc+1)*512], "sLd")
                        xr_ready = e.c["sLd"]
                        e.wait("vector", "sPe", womm)
                        e.wait("vector", "sLd", xr_ready)
                        e.wait("vector", "sSt", smf_st[ti])
                        e.wait("vector", "sAc", smf_ac[ti])
                        e.vop("tensor_tensor", sm_f[:, ti*512:(ti+1)*512],
                              PSf[:, bank*512:(bank+1)*512],
                              sm2_f[:, ti*512:(ti+1)*512], ALU.add, sem="sVe")
                        wo_psum_free[bank] = e.c["sVe"]
                        e.wait("vector", "sVe", e.c["sVe"])
                        e.vop("tensor_tensor", sm_f[:, ti*512:(ti+1)*512],
                              sm_f[:, ti*512:(ti+1)*512],
                              bos[:, jc*512:(jc+1)*512], ALU.add, sem="sVe")
                        tmp_v = e.c["sVe"]
                        e.wait("vector", "sVe", tmp_v)
                        smf_ve[ti] = e.vop(
                            "reduce_sum", st_f[:, hh*8+jc:hh*8+jc+1],
                            sm_f[:, ti*512:(ti+1)*512], axis=AX.X, sem="sVe")
                        e.wait("scalar", "sVe", tmp_v)
                        smf_ac[ti] = e.act(
                            sm2_f[:, ti*512:(ti+1)*512],
                            sm_f[:, ti*512:(ti+1)*512], AF.Square,
                            accum=st_f[:, 64+hh*8+jc:64+hh*8+jc+1], sem="sAc")
                        e.wait("sync", "sVe", tmp_v)
                        smf_st[ti] = e.dma(
                            "sync", x1pre[rl0:rl0+128, jc*512:(jc+1)*512],
                            sm_f[:, ti*512:(ti+1)*512], "sSt")
                        x1p_store[(p, hh, jc)] = smf_st[ti]

                        if jc == 7:
                            stats_v = e.c["sVe"]
                            stats_a = e.c["sAc"]
                            e.wait("vector", "sVe", stats_v)
                            e.wait("vector", "sAc", stats_a)
                            e.vop("reduce_sum", st_f[:, 128+hh:129+hh],
                                  st_f[:, hh*8:hh*8+8], axis=AX.X, sem="sVe")
                            e.vop("reduce_sum", st_f[:, 136+hh:137+hh],
                                  st_f[:, 64+hh*8:64+hh*8+8], axis=AX.X,
                                  sem="sVe")
                            e.wait("vector", "sVe", e.c["sVe"])
                            e.vop("tensor_scalar", st_f[:, 144+hh:145+hh],
                                  st_f[:, 128+hh:129+hh], 1.0/4096.0, None,
                                  ALU.mult, sem="sVe")
                            e.wait("vector", "sVe", e.c["sVe"])
                            e.vop("tensor_tensor", st_f[:, 152+hh:153+hh],
                                  st_f[:, 144+hh:145+hh],
                                  st_f[:, 144+hh:145+hh], ALU.mult, sem="sVe")
                            e.wait("vector", "sVe", e.c["sVe"])
                            e.vop("scalar_tensor_tensor",
                                  st_f[:, 160+hh:161+hh],
                                  st_f[:, 136+hh:137+hh], 1.0/4096.0,
                                  st_f[:, 152+hh:153+hh], ALU.mult,
                                  ALU.subtract, sem="sVe")
                            var_v = e.c["sVe"]
                            e.wait("scalar", "sVe", var_v)
                            e.act(st_f[:, 168+hh:169+hh],
                                  st_f[:, 160+hh:161+hh], AF.Ln,
                                  bias=cst[:, 2:3], scale=1.0, sem="sAc")
                            e.wait("scalar", "sAc", e.c["sAc"])
                            e.act(st_f[:, 176+hh:177+hh],
                                  st_f[:, 168+hh:169+hh], AF.Exp,
                                  bias=cst[:, 5:6], scale=-0.5, sem="sAc")
                            rstd_a = e.c["sAc"]
                            for jc2 in range(8):
                                t2 = jc2 % 2
                                if jc2 == 0:
                                    e.wait("vector", "sGc", x1bf_stores)
                                e.wait("gpsimd", "sSt", x1p_store[(p, hh, jc2)])
                                e.wait("gpsimd", "sVe", nrm_done[t2])
                                if p == 0 and hh == 0 and jc2 == 0:
                                    e.wait("gpsimd", "sPe", wo_pe_start)
                                e.wait("gpsimd", "sGc", x1f_stores)
                                e.dma("gpsimd",
                                      Gf[:, prb_f0+t2*512:prb_f0+(t2+1)*512],
                                      x1pre[rl0:rl0+128, jc2*512:(jc2+1)*512],
                                      "sGc")
                                ch_ready = e.c["sGc"]
                                e.wait("vector", "sGc", ch_ready)
                                e.wait("vector", "sAc", rstd_a)
                                e.vop("tensor_scalar",
                                      G[:, x1bf_tile0+jc2*512:x1bf_tile0+(jc2+1)*512],
                                      Gf[:, prb_f0+t2*512:prb_f0+(t2+1)*512],
                                      st_f[:, 144+hh:145+hh],
                                      st_f[:, 176+hh:177+hh],
                                      ALU.subtract, ALU.mult, sem="sVe")
                                nrm_done[t2] = e.vop(
                                    "tensor_scalar",
                                    Gf[:, prb_f0+t2*512:prb_f0+(t2+1)*512],
                                    Gf[:, prb_f0+t2*512:prb_f0+(t2+1)*512],
                                    st_f[:, 144+hh:145+hh],
                                    st_f[:, 176+hh:177+hh],
                                    ALU.subtract, ALU.mult, sem="sVe")
                                e.wait("gpsimd", "sVe", nrm_done[t2])
                                x1f_stores = e.dma(
                                    "gpsimd",
                                    x1f_d[rl0:rl0+128, jc2*512:(jc2+1)*512],
                                    Gf[:, prb_f0+t2*512:prb_f0+(t2+1)*512],
                                    "sGc")
                            bf_v = e.c["sVe"]
                            e.wait("gpsimd", "sVe", bf_v)
                            x1bf_stores = e.dma(
                                "gpsimd", x1bf_d[rl0:rl0+128, :],
                                G[:, x1bf_tile0:x1bf_tile0+4096], "sGc")
                    if jc + 2 < 8:
                        e.wait("sync", "sPe", wo_strip_free[wbuf])
                        e.dma("sync", G[:, wb0:wb0+16384],
                              WOS[:, (jc+2)*16384:(jc+3)*16384], "sLd")
                        wot_ready[jc + 2] = e.c["sLd"]
                sweep_done_v = e.c["sVe"]
                sweep_done_st = e.c["sGc"]
                x1bf_pass_store[p] = e.c["sGc"]
                qbank_free[0] = max(qbank_free[0], wo_psum_free[0])
                qbank_free[1] = max(qbank_free[1], wo_psum_free[1])
                wqt_free = max(wqt_free, wo_strip_free[0])
                xtt_half_free[0] = max(xtt_half_free[0], wo_strip_free[1])
                xtt_half_free[1] = max(xtt_half_free[1], wo_strip_free[1])

                if p == 1:
                    x1src_ready = e.c["sGc"]
                    for cb in range(32):
                        dst = X1Tr.rearrange(
                            "p th (cb f) -> p th cb f", f=1024)[:, 1, cb, :]
                        x1ts_done[1] = pretr(
                            x1bf_d[1024:2048, cb*128:(cb+1)*128],
                            dst, x1src_ready, store_q="gpsimd")

            if stage == "wo":
                e.wait("gpsimd", "sGc", sweep_done_st)
                for i in range(8):
                    e.dma("gpsimd", dbg["x1f"][i*256:(i+1)*256, :],
                          x1f_d[i*256:(i+1)*256, :], "sGc")
                e.wait("sync", "sGc", e.c["sGc"])
                return

            ic_segs = []
            for ic in range(14):
                segs = []
                i0 = ic * 512
                while i0 < (ic + 1) * 512:
                    ex = i0 // INTER
                    i1 = min((ex + 1) * INTER, (ic + 1) * 512)
                    segs.append((i0 - ic * 512, i1 - ic * 512, ex))
                    i0 = i1
                ic_segs.append(segs)

            gu_grp_free = [0] * 4
            x1t_free = 0
            hw_ic_last = {}
            gp_free = [sweep_done_v, sweep_done_v]
            silu_free = [0, 0]
            silu_rd = [0, 0]
            hw_st = [0] * 4
            hw_ve = [0] * 4
            for th in range(2):
                e.waitt("sync", x1ts_done[th])
                e.wait("sync", "sPe", x1t_free)
                e.dma("sync", G[:, 0:32768],
                      X1TS[:, th*32768:(th+1)*32768], "sLd")
                x1t_ready = e.c["sLd"]

                e.wait("tensor", "sLd", x1t_ready)
                if th == 0:
                    e.wait("tensor", "sGc", sweep_done_st)
                    e.wait("tensor", "sAc", post_attn_ac)
                    e.wait("tensor", "sVe", last_av_free)
                for cb in range(32):
                    e.mm(PSf[0:16, 3072:3584],
                         gwt_sb[:, cb*16:(cb+1)*16],
                         G[:, cb*1024:cb*1024+512],
                         cb == 0, cb == 31, sem=None)
                    e.mm(PSf[0:16, 3584:4096],
                         gwt_sb[:, cb*16:(cb+1)*16],
                         G[:, cb*1024+512:(cb+1)*1024],
                         cb == 0, cb == 31,
                         sem="sPe" if cb == 31 else None)
                lgT_p = e.c["sPe"]
                e.wait("vector", "sPe", lgT_p)
                e.vop("tensor_copy", sm2_f[0:16, 0:1024],
                      PSf[0:16, 3072:4096], sem="sVe")
                lgT_v = e.c["sVe"]
                for tb in range(8):
                    b = tb % 2
                    pf0 = (6 + b) * 512
                    e.wait("tensor", "sVe", max(lgT_v, tb_free[b]))
                    e.mm(PSf[:, pf0:pf0+16],
                         sm2_f[0:16, tb*128:(tb+1)*128], eyef[0:16, :],
                         True, True, sem="sPe")
                    e.wait("vector", "sPe", e.c["sPe"])
                    tb_free[b] = e.vop(
                        "tensor_copy", lg_f[:, (th*8+tb)*16:(th*8+tb+1)*16],
                        PSf[:, pf0:pf0+16], sem="sVe")
                for tb in range(8):
                    tt = th * 8 + tb
                    L = lg_f[:, tt*16:(tt+1)*16]
                    W = w_f[:, tt*16:(tt+1)*16]
                    e.wait("vector", "sVe", e.c["sVe"])
                    e.vop("reduce_max", tk_f[:, 0:1], L, axis=AX.X, sem="sVe")
                    e.wait("vector", "sVe", e.c["sVe"])
                    e.vop("tensor_scalar", tk_f[:, 1:2], tk_f[:, 0:1],
                          -1.0, None, ALU.mult, sem="sVe")
                    e.vop("tensor_scalar", tk_f[:, 16:32], L,
                          tk_f[:, 0:1], None, ALU.is_equal, sem="sVe")
                    e.wait("vector", "sVe", e.c["sVe"])
                    e.vop("scalar_tensor_tensor", tk_f[:, 32:48],
                          tk_f[:, 16:32], -1e30, L, ALU.mult, ALU.add,
                          sem="sVe")
                    e.wait("vector", "sVe", e.c["sVe"])
                    e.vop("reduce_max", tk_f[:, 2:3], tk_f[:, 32:48],
                          axis=AX.X, sem="sVe")
                    e.wait("vector", "sVe", e.c["sVe"])
                    e.vop("tensor_scalar", tk_f[:, 16:32], L,
                          tk_f[:, 2:3], None, ALU.is_ge, sem="sVe")
                    ge_v = e.c["sVe"]
                    e.wait("scalar", "sVe", ge_v)
                    e.act(tk_f[:, 48:64], L, AF.Exp,
                          bias=tk_f[:, 1:2], scale=1.0, sem="sAc")
                    e.wait("vector", "sAc", e.c["sAc"])
                    e.vop("tensor_tensor", tk_f[:, 32:48],
                          tk_f[:, 48:64], tk_f[:, 16:32], ALU.mult, sem="sVe")
                    e.wait("vector", "sVe", e.c["sVe"])
                    e.vop("reduce_sum", tk_f[:, 3:4], tk_f[:, 32:48],
                          axis=AX.X, sem="sVe")
                    e.wait("vector", "sVe", e.c["sVe"])
                    e.vop("reciprocal", tk_f[:, 4:5], tk_f[:, 3:4], sem="sVe")
                    e.wait("vector", "sVe", e.c["sVe"])
                    e.vop("tensor_scalar", W, tk_f[:, 32:48],
                          tk_f[:, 4:5], None, ALU.mult, sem="sVe")
                rt_v = e.c["sVe"]

                for ic in range(14):
                    for grp in range(4):
                        e.waitt("sync", gus_done_g)
                        e.waitt("sync", gus_done)
                        e.wait("sync", "sPe", gu_grp_free[grp])
                        e.dma("sync",
                              G[:, WQT0+grp*4096:WQT0+(grp+1)*4096],
                              GUSg[:, ic*16384+grp*4096:ic*16384+(grp+1)*4096],
                              "sLd")
                        e.dma("sync",
                              G[:, XTT0+grp*4096:XTT0+(grp+1)*4096],
                              GUSu[:, ic*16384+grp*4096:ic*16384+(grp+1)*4096],
                              "sLd")
                    gu_ready = e.c["sLd"]
                    e.wait("tensor", "sLd", gu_ready)
                    for tt8 in range(8):
                        tt = th * 8 + tt8
                        pb = tt8 % 2
                        gp0 = pb * 1024
                        up0 = pb * 1024 + 512
                        e.wait("tensor", "sVe", gp_free[pb])
                        e.wait("tensor", "sAc", silu_free[pb])
                        for cb in range(32):
                            e.mm(PSf[:, gp0:gp0+512],
                                 G[:, cb*1024+tt8*128:cb*1024+(tt8+1)*128],
                                 G[:, WQT0+cb*512:WQT0+(cb+1)*512],
                                 cb == 0, cb == 31, sem=None)
                            e.mm(PSf[:, up0:up0+512],
                                 G[:, cb*1024+tt8*128:cb*1024+(tt8+1)*128],
                                 G[:, XTT0+cb*512:XTT0+(cb+1)*512],
                                 cb == 0, cb == 31,
                                 sem="sPe" if cb == 31 else None)
                            if tt8 == 7 and cb % 8 == 7:
                                gu_grp_free[cb // 8] = e.c["sPe"]
                        gud = e.c["sPe"]
                        hbuf = (ic * 8 + tt8) % 4
                        e.wait("scalar", "sPe", gud)
                        e.wait("scalar", "sVe", silu_rd[pb])
                        silu_free[pb] = e.act(
                            sm2b[:, pb*512+2048:pb*512+2560],
                            PSf[:, gp0:gp0+512], AF.Silu, sem="sAc")
                        e.wait("vector", "sAc", silu_free[pb])
                        e.wait("vector", "sSt", hw_st[hbuf])
                        e.wait("vector", "sPe", gud)
                        for (o0, o1, ex) in ic_segs[ic]:
                            e.vop("scalar_tensor_tensor",
                                  smb[:, hbuf*1024+o0:hbuf*1024+o1],
                                  sm2b[:, pb*512+2048+o0:pb*512+2048+o1],
                                  w_f[:, tt*16+ex:tt*16+ex+1],
                                  PSf[:, up0+o0:up0+o1],
                                  ALU.mult, ALU.mult, sem="sVe")
                        gp_free[pb] = e.c["sVe"]
                        silu_rd[pb] = e.c["sVe"]
                        hw_ve[hbuf] = e.c["sVe"]
                        e.wait("sync", "sVe", hw_ve[hbuf])
                        hw_st[hbuf] = e.dma(
                            "sync", hw_d[tt*128:(tt+1)*128, ic*512:(ic+1)*512],
                            smb[:, hbuf*1024:hbuf*1024+512], "sSt")
                        if th == 1 and tt8 == 7:
                            hw_ic_last[ic] = hw_st[hbuf]
                x1t_free = e.c["sPe"]
            hw_stores = e.c["sSt"]

            hws_done = []
            for ib in range(56):
                hws_done.append(pretr(hw_d[0:2048, ib*128:(ib+1)*128],
                                      HWS[:, ib*2048:(ib+1)*2048],
                                      ("sSt", hw_ic_last[ib // 4]),
                                      store_q="gpsimd"))

            if stage == "ffn1":
                e.dma("sync", dbg["w"], w_f[:, :], "sSt")
                e.wait("gpsimd", "sSt", hw_stores)
                for i in range(8):
                    e.dma("gpsimd", dbg["hw"][i*256:(i+1)*256, :],
                          hw_d[i*256:(i+1)*256, :], "sGc")
                for i in range(8):
                    e.dma("gpsimd", dbg["x1f"][i*256:(i+1)*256, :],
                          x1f_d[i*256:(i+1)*256, :], "sGc")
                e.wait("sync", "sGc", e.c["sGc"])
                e.wait("sync", "sSt", e.c["sSt"])
                return

            SUP = 4
            NSUP = 14
            f1_v = e.c["sVe"]
            f1_a = e.c["sAc"]
            e.wait("tensor", "sVe", f1_v)
            e.wait("tensor", "sAc", f1_a)
            cacc_done = [0, 0]
            set_free = [0, 0]
            cp_st = 0
            for q in range(4):
                c0 = q * 1024
                for sp in range(NSUP):
                    ib0 = sp * SUP
                    sset = sp % 2
                    s0 = 32768 + sset * 12288
                    e.waitt("sync", dts_done[-1])
                    e.waitt("sync", hws_done[ib0 + SUP - 1])
                    e.wait("sync", "sPe", set_free[sset])
                    e.dma("sync", G[:, s0:s0+4096],
                          DTS[:, q*57344+ib0*1024:q*57344+(ib0+SUP)*1024],
                          "sLd")
                    e.dma("sync", G[:, s0+4096:s0+12288],
                          HWS[:, ib0*2048:(ib0+SUP)*2048], "sLd")
                    strips_ready = e.c["sLd"]
                    e.wait("tensor", "sLd", strips_ready)
                    for tt in range(16):
                        pg = tt % 2
                        p0 = pg * 1024
                        e.wait("tensor", "sVe", cacc_done[pg])
                        for k in range(SUP):
                            for nq in range(2):
                                e.mm(PSf[:, p0+nq*512:p0+(nq+1)*512],
                                     G[:, s0+4096+k*2048+tt*128:
                                          s0+4096+k*2048+(tt+1)*128],
                                     G[:, s0+k*1024+nq*512:s0+k*1024+(nq+1)*512],
                                     k == 0, k == SUP - 1,
                                     sem="sPe" if (k == SUP - 1 and nq == 1)
                                     else None)
                        dmm = e.c["sPe"]
                        if tt == 15:
                            set_free[sset] = dmm
                        e.wait("vector", "sPe", dmm)
                        if q == 0 and sp == 0:
                            e.wait("vector", "sPe", x1t_free)
                        if sp == 0:
                            e.wait("vector", "sSt", cp_st)
                            cacc_done[pg] = e.vop(
                                "tensor_copy", Gf[:, tt*1024:(tt+1)*1024],
                                PSf[:, p0:p0+1024], sem="sVe")
                        else:
                            cacc_done[pg] = e.vop(
                                "tensor_tensor",
                                Gf[:, tt*1024:(tt+1)*1024],
                                Gf[:, tt*1024:(tt+1)*1024],
                                PSf[:, p0:p0+1024], ALU.add, sem="sVe")
                if q < 3:
                    e.wait("sync", "sVe", max(cacc_done))
                    for tt in range(16):
                        e.dma("sync", cp0[tt*128:(tt+1)*128, q*1024:(q+1)*1024],
                              Gf[:, tt*1024:(tt+1)*1024], "sSt")
                    cp_st = e.c["sSt"]

            X1R = 16384
            CPT = 20480
            X2A = 23552
            SQD = 27648
            fin_v = 0
            fin_st = 0
            f2_pe = e.c["sPe"]
            for tt in range(16):
                r0 = tt * 128
                if tt == 0:
                    e.wait("sync", "sPe", f2_pe)
                e.wait("sync", "sVe", fin_v)
                e.wait("sync", "sSt", fin_st)
                e.dma("sync", Gf[:, X1R:X1R+4096], x1f_d[r0:r0+128, :], "sLd")
                e.dma("sync", Gf[:, CPT:CPT+3072], cp0[r0:r0+128, :], "sLd")
                fr = e.c["sLd"]
                e.wait("vector", "sLd", fr)
                e.wait("vector", "sVe", fin_v)
                e.vop("tensor_tensor", Gf[:, X2A:X2A+3072],
                      Gf[:, CPT:CPT+3072], Gf[:, X1R:X1R+3072], ALU.add,
                      sem="sVe")
                e.vop("tensor_tensor", Gf[:, X2A+3072:X2A+4096],
                      Gf[:, tt*1024:(tt+1)*1024], Gf[:, X1R+3072:X1R+4096],
                      ALU.add, sem="sVe")
                x2p_v = e.c["sVe"]
                e.wait("vector", "sVe", x2p_v)
                e.vop("reduce_sum", tk_f[:, 8:9], Gf[:, X2A:X2A+4096],
                      axis=AX.X, sem="sVe")
                s1v = e.c["sVe"]
                e.wait("scalar", "sVe", x2p_v)
                e.act(Gf[:, SQD:SQD+4096], Gf[:, X2A:X2A+4096], AF.Square,
                      accum=tk_f[:, 10:11], sem="sAc")
                s2a = e.c["sAc"]
                e.wait("vector", "sVe", s1v)
                e.wait("vector", "sAc", s2a)
                e.vop("tensor_scalar", tk_f[:, 14:15], tk_f[:, 8:9],
                      1.0/4096.0, None, ALU.mult, sem="sVe")
                e.wait("vector", "sVe", e.c["sVe"])
                e.vop("tensor_tensor", tk_f[:, 15:16], tk_f[:, 14:15],
                      tk_f[:, 14:15], ALU.mult, sem="sVe")
                e.wait("vector", "sVe", e.c["sVe"])
                e.vop("scalar_tensor_tensor", tk_f[:, 20:21],
                      tk_f[:, 10:11], 1.0/4096.0, tk_f[:, 15:16],
                      ALU.mult, ALU.subtract, sem="sVe")
                varv = e.c["sVe"]
                e.wait("scalar", "sVe", varv)
                e.act(tk_f[:, 21:22], tk_f[:, 20:21], AF.Ln,
                      bias=cst[:, 2:3], scale=1.0, sem="sAc")
                e.wait("scalar", "sAc", e.c["sAc"])
                e.act(tk_f[:, 22:23], tk_f[:, 21:22], AF.Exp,
                      bias=cst[:, 5:6], scale=-0.5, sem="sAc")
                rsta = e.c["sAc"]
                e.wait("vector", "sAc", rsta)
                e.vop("tensor_scalar", Gf[:, X2A:X2A+4096],
                      Gf[:, X2A:X2A+4096], tk_f[:, 14:15],
                      tk_f[:, 22:23], ALU.subtract, ALU.mult, sem="sVe")
                fin_v = e.c["sVe"]
                e.wait("sync", "sVe", fin_v)
                fin_st = e.dma("sync", out[r0:r0+128, :],
                               Gf[:, X2A:X2A+4096], "sSt")
            e.wait("sync", "sSt", e.c["sSt"])

        block.sync(lambda eng: emit("sync", eng))
        block.gpsimd(lambda eng: emit("gpsimd", eng))
        block.tensor(lambda eng: emit("tensor", eng))
        block.vector(lambda eng: emit("vector", eng))
        block.scalar(lambda eng: emit("scalar", eng))
    return nc


def _mk_inputs(inputs):
    x = np.ascontiguousarray(np.asarray(inputs["L0_attn_hidden_state"]))
    Wq = np.asarray(inputs["L0_attn_Wq"]); bq = np.asarray(inputs["L0_attn_bq"])
    Wk = np.asarray(inputs["L0_attn_Wk"]); bk = np.asarray(inputs["L0_attn_bk"])
    Wv = np.asarray(inputs["L0_attn_Wv"]); bv = np.asarray(inputs["L0_attn_bv"])
    Wo = np.asarray(inputs["L0_attn_Wo"]); bo = np.asarray(inputs["L0_attn_bo"])
    consts = np.zeros((128, 8), dtype=np.float32)
    consts[:, 0] = EXP_SHIFT
    consts[:, 2] = EPS
    eye_bf = np.eye(128, dtype=ml_dtypes.bfloat16)
    eye_f = np.zeros((128, 16), dtype=np.float32)
    eye_f[:16, :16] = np.eye(16, dtype=np.float32)
    bo_bc = np.ascontiguousarray(
        np.broadcast_to(bo.astype(np.float32)[None, :], (128, CD)))
    gm = np.ascontiguousarray(np.asarray(inputs["L0_ffn_g"]))
    um = np.ascontiguousarray(np.asarray(inputs["L0_ffn_u"]))
    dm = np.ascontiguousarray(np.asarray(inputs["L0_ffn_d"]))
    gwm = np.ascontiguousarray(np.asarray(inputs["L0_ffn_gate_w"]))
    Wk = np.ascontiguousarray(Wk); Wv = np.ascontiguousarray(Wv)
    Wo = np.ascontiguousarray(Wo)
    maps = []
    for b in range(4):
        for g in range(2):
            maps.append(dict(
                x_b=x[b],
                xr_b=np.ascontiguousarray(x[b, 2048*g:2048*(g+1), :]),
                wq=np.ascontiguousarray(Wq[2048*g:2048*(g+1)]),
                bq2=np.ascontiguousarray(
                    bq[2048*g:2048*(g+1)].reshape(16, 128).T),
                wk=Wk, bk2=np.ascontiguousarray(bk.reshape(128, 1)),
                wv=Wv, bv2=np.ascontiguousarray(bv.reshape(128, 1)),
                wo=Wo, bo_bc=bo_bc, gwi=gwm,
                gmat=gm, umat=um, dmat=dm,
                consts=consts, eye_bf=eye_bf, eye_f=eye_f,
            ))
    return maps


def kernel(**inputs):
    maps = _mk_inputs(inputs)
    nc = build_nc()
    res = run_bass_kernel_spmd(nc, maps, core_ids=list(range(8)))
    full = np.empty((4, S, CD), dtype=np.float32)
    for b in range(4):
        for g in range(2):
            full[b, 2048*g:2048*(g+1), :] = res.results[b*2+g]["out"]
    return full


# version 06 (final)
# speedup vs baseline: 1.1042x; 1.1042x over previous
import numpy as np
import ml_dtypes
from contextlib import ExitStack

import concourse.bass as bass
import concourse.mybir as mybir
from concourse.bass_utils import run_bass_kernel_spmd

F32 = mybir.dt.float32
BF16 = mybir.dt.bfloat16
AF = mybir.ActivationFunctionType
ALU = mybir.AluOpType
AX = mybir.AxisListType

S = 4096
CD = 4096
NI = 7168
INTER = 448
TOK = 2048
SCALE = 1.0 / float(np.sqrt(128.0))
EXP_SHIFT = -12.0
EPS = 1e-5

STAGE = "full"

SEMS = ["sLd", "sSt", "sSc", "sGc", "sPe", "sAc", "sVe", "sGp"]

QT0 = 0
WQT0 = 32768
XTT0 = 49152
KT0 = 65536
VA0 = 69632
WKT0 = 73760
WVT0 = 77856
G1_COLS = 81952
PRB0 = WKT0


class E:

    def __init__(self, nc, active, eng, sems):
        self.nc = nc
        self.a = active
        self.e = eng
        self.sems = sems
        self.c = {k: 0 for k in sems}

    def dma(self, q, out, in_, sem, tr=False):
        self.c[sem] += 16
        if self.a == q:
            if tr:
                self.e.dma_start_transpose(out, in_).then_inc(self.sems[sem], 16)
            else:
                self.e.dma_start(out=out, in_=in_).then_inc(self.sems[sem], 16)
        return self.c[sem]

    def wait(self, engname, sem, val):
        if val > self.c[sem]:
            raise RuntimeError(f"wait {sem}>={val} but max {self.c[sem]}")
        if val > 0 and self.a == engname:
            self.e.wait_ge(self.sems[sem], val)

    def waitt(self, engname, tup):
        if tup is not None and tup[0] is not None:
            self.wait(engname, tup[0], tup[1])

    def _fin(self, ins, sem):
        if sem is not None:
            self.c[sem] += 1
            if ins is not None:
                ins.then_inc(self.sems[sem], 1)
        return self.c[sem] if sem else None

    def mm(self, out, lhsT, rhs, start, stop, sem=None):
        ins = None
        if self.a == "tensor":
            ins = self.e.matmul(out, lhsT, rhs, start=start, stop=stop)
        return self._fin(ins, sem)

    def tr(self, out, in_, ident, sem=None):
        ins = None
        if self.a == "tensor":
            ins = self.e.transpose(out, in_, ident)
        return self._fin(ins, sem)

    def act(self, out, in_, func, bias=0.0, scale=1.0, accum=None, sem=None):
        ins = None
        if self.a == "scalar":
            ins = self.e.activation(out, in_, func, bias=bias, scale=scale,
                                    accum_out=accum)
        return self._fin(ins, sem)

    def vop(self, name, *args, sem=None, eng="vector", **kw):
        ins = None
        if self.a == eng:
            ins = getattr(self.e, name)(*args, **kw)
        return self._fin(ins, sem)


def build_nc(stage=None):
    stage = stage or STAGE
    nc = bass.Bass("TRN2", target_bir_lowering=False, debug=False)

    x_b = nc.dram_tensor("x_b", [S, CD], F32, kind="ExternalInput").ap()
    xr_b = nc.dram_tensor("xr_b", [TOK, CD], F32, kind="ExternalInput").ap()
    wq = nc.dram_tensor("wq", [2048, CD], F32, kind="ExternalInput").ap()
    bq2 = nc.dram_tensor("bq2", [128, 16], F32, kind="ExternalInput").ap()
    wk = nc.dram_tensor("wk", [128, CD], F32, kind="ExternalInput").ap()
    bk2 = nc.dram_tensor("bk2", [128, 1], F32, kind="ExternalInput").ap()
    wv = nc.dram_tensor("wv", [128, CD], F32, kind="ExternalInput").ap()
    bv2 = nc.dram_tensor("bv2", [128, 1], F32, kind="ExternalInput").ap()
    wo = nc.dram_tensor("wo", [CD, CD], F32, kind="ExternalInput").ap()
    bo_bc = nc.dram_tensor("bo_bc", [128, CD], F32, kind="ExternalInput").ap()
    gwi = nc.dram_tensor("gwi", [16, CD], F32, kind="ExternalInput").ap()
    gmat = nc.dram_tensor("gmat", [NI, CD], F32, kind="ExternalInput").ap()
    umat = nc.dram_tensor("umat", [NI, CD], F32, kind="ExternalInput").ap()
    dmat = nc.dram_tensor("dmat", [CD, NI], F32, kind="ExternalInput").ap()
    consts = nc.dram_tensor("consts", [128, 8], F32, kind="ExternalInput").ap()
    eye_bf_d = nc.dram_tensor("eye_bf", [128, 128], BF16, kind="ExternalInput").ap()
    eye_f_d = nc.dram_tensor("eye_f", [128, 16], F32, kind="ExternalInput").ap()

    out = nc.dram_tensor("out", [TOK, CD], F32, kind="ExternalOutput").ap()
    dbg = {}
    if stage != "full":
        dbg["qt"] = nc.dram_tensor("dbg_qt", [128, 32768], BF16, kind="ExternalOutput").ap()
        dbg["kt"] = nc.dram_tensor("dbg_kt", [128, 4096], BF16, kind="ExternalOutput").ap()
        dbg["va"] = nc.dram_tensor("dbg_va", [128, 4128], BF16, kind="ExternalOutput").ap()
        dbg["x1f"] = nc.dram_tensor("dbg_x1f", [TOK, CD], F32, kind="ExternalOutput").ap()
        dbg["hw"] = nc.dram_tensor("dbg_hw", [TOK, NI], BF16, kind="ExternalOutput").ap()
        dbg["w"] = nc.dram_tensor("dbg_w", [128, 256], F32, kind="ExternalOutput").ap()

    x_bf = nc.dram_tensor("x_bf", [S, CD], BF16, kind="Internal").ap()
    wq_bf = nc.dram_tensor("wq_bf", [2048, CD], BF16, kind="Internal").ap()
    wo_bf = nc.dram_tensor("wo_bf", [CD, CD], BF16, kind="Internal").ap()
    g_bf = nc.dram_tensor("g_bf", [NI, CD], BF16, kind="Internal").ap()
    u_bf = nc.dram_tensor("u_bf", [NI, CD], BF16, kind="Internal").ap()
    d_bf = nc.dram_tensor("d_bf", [CD, NI], BF16, kind="Internal").ap()
    x1pre = nc.dram_tensor("x1pre", [TOK, CD], F32, kind="Internal").ap()
    x1bf_d = nc.dram_tensor("x1bf_d", [TOK, CD], BF16, kind="Internal").ap()
    x1f_d = nc.dram_tensor("x1f_d", [TOK, CD], F32, kind="Internal").ap()
    hw_d = nc.dram_tensor("hw_d", [TOK, NI], BF16, kind="Internal").ap()
    cp0 = nc.dram_tensor("cp0", [TOK, 3072], F32, kind="Internal").ap()

    XTS = nc.dram_tensor("XTS", [128, 8 * 16384], BF16, kind="Internal").ap()
    WQS = nc.dram_tensor("WQS", [128, 4 * 16384], BF16, kind="Internal").ap()
    WOS = nc.dram_tensor("WOS", [128, 8 * 16384], BF16, kind="Internal").ap()
    GUSg = nc.dram_tensor("GUSg", [128, 14 * 16384], BF16, kind="Internal").ap()
    GUSu = nc.dram_tensor("GUSu", [128, 14 * 16384], BF16, kind="Internal").ap()
    DTS = nc.dram_tensor("DTS", [128, 4 * 56 * 1024], BF16, kind="Internal").ap()
    HWS = nc.dram_tensor("HWS", [128, 56 * 2048], BF16, kind="Internal").ap()
    X1TS = nc.dram_tensor("X1TS", [128, 2 * 32768], BF16, kind="Internal").ap()

    G1 = nc.alloc_sbuf_tensor("G1", [128, G1_COLS], BF16)
    TRB = nc.alloc_sbuf_tensor("TRB", [128, 4096], BF16)
    sm_f = nc.alloc_sbuf_tensor("sm_f", [128, 2048], F32)
    sm2_f = nc.alloc_sbuf_tensor("sm2_f", [128, 2048], F32)
    st_f = nc.alloc_sbuf_tensor("st_f", [128, 272], F32)
    lg_f = nc.alloc_sbuf_tensor("lg_f", [128, 256], F32)
    w_f = nc.alloc_sbuf_tensor("w_f", [128, 256], F32)
    tk_f = nc.alloc_sbuf_tensor("tk_f", [128, 64], F32)
    gwt_sb = nc.alloc_sbuf_tensor("gwt_sb", [128, 512], BF16)
    bqs = nc.alloc_sbuf_tensor("bqs", [128, 16], F32)
    bks = nc.alloc_sbuf_tensor("bks", [128, 1], F32)
    bvs = nc.alloc_sbuf_tensor("bvs", [128, 1], F32)
    cst = nc.alloc_sbuf_tensor("cst", [128, 8], F32)
    eyeb = nc.alloc_sbuf_tensor("eyeb", [128, 128], BF16)
    eyef = nc.alloc_sbuf_tensor("eyef", [128, 16], F32)
    bos = nc.alloc_sbuf_tensor("bos", [128, CD], F32)
    PS = nc.alloc_psum_tensor("PS", [128, 4096], F32)

    G = G1[:, :]
    Gf = G1[:, :].bitcast(F32)
    PSf = PS[:, :]
    PSb = PS[:, :].bitcast(BF16)
    smb = sm_f[:, :].bitcast(BF16)
    sm2b = sm2_f[:, :].bitcast(BF16)
    lgb = lg_f[:, :].bitcast(BF16)
    wfb = w_f[:, :].bitcast(BF16)

    with ExitStack() as ctx:
        sems = {n: ctx.enter_context(nc.semaphore(n)) for n in SEMS}
        block = ctx.enter_context(nc.Block())

        def emit(active, eng):
            e = E(nc, active, eng, sems)

            e.dma("sync", cst[:, :], consts, "sLd")
            e.dma("sync", eyeb[:, :], eye_bf_d, "sLd")
            e.dma("sync", eyef[:, :], eye_f_d, "sLd")
            e.dma("sync", bqs[:, :], bq2, "sLd")
            e.dma("sync", bks[:, :], bk2, "sLd")
            e.dma("sync", bvs[:, :], bv2, "sLd")
            e.dma("sync", bos[:, :], bo_bc, "sLd")
            c_consts = e.c["sLd"]

            e.dma("gpsimd", G[:, XTT0:XTT0+4096], wk, "sGc")
            e.dma("gpsimd", G[:, XTT0+4096:XTT0+8192], wv, "sGc")
            kv_cast = e.dma("gpsimd", G[0:16, XTT0+8192:XTT0+12288], gwi, "sGc")
            x_cast = []
            for i in range(8):
                x_cast.append(e.dma("gpsimd", x_bf[i*256:(i+1)*256, :],
                                    x_b[i*256:(i+1)*256, :], "sGc"))
            wq_cast = []
            for i in range(8):
                wq_cast.append(e.dma("gpsimd", wq_bf[i*256:(i+1)*256, :],
                                     wq[i*256:(i+1)*256, :], "sGc"))
            for i in range(8, 16):
                x_cast.append(e.dma("gpsimd", x_bf[i*256:(i+1)*256, :],
                                    x_b[i*256:(i+1)*256, :], "sGc"))
            e.wait("vector", "sLd", c_consts)
            e.vop("memset", G[:, VA0:VA0+4128], 1.0, sem="sVe")
            va_ones = e.c["sVe"]

            bounce4 = [("t0", TRB[:, 0:2048]), ("t1", TRB[:, 2048:4096]),
                       ("s0", smb[:, 0:2048]), ("s1", sm2b[:, 0:2048])]
            bounce2 = bounce4[:2]
            trb_state = {"bufs": bounce4, "free": {}, "i": 0}

            def pretr(src, dst, cast_wait, store_q="scalar"):
                bufs = trb_state["bufs"]
                key, buf = bufs[trb_state["i"] % len(bufs)]
                trb_state["i"] += 1
                R = src.shape[0]
                store_sem = "sSc" if store_q == "scalar" else "sGc"
                if isinstance(cast_wait, tuple):
                    e.wait("sync", cast_wait[0], cast_wait[1])
                else:
                    e.wait("sync", "sGc", cast_wait)
                fs = trb_state["free"].get(key)
                if fs is not None:
                    e.wait("sync", fs[0], fs[1])
                e.dma("sync", buf[:, 0:R], src, "sLd", tr=True)
                tr_done = e.c["sLd"]
                e.wait(store_q, "sLd", tr_done)
                cnt = e.dma(store_q, dst, buf[:, 0:R], store_sem)
                trb_state["free"][key] = (store_sem, cnt)
                return (store_sem, cnt)

            XTSr = XTS.rearrange("p (ss f) -> p ss f", f=16384)
            xts_done = [None, None]
            def emit_xts(rh):
                for cb in range(32):
                    dst = XTSr[:, rh*4:(rh+1)*4, :].rearrange(
                        "p ss (cb f) -> p ss cb f", f=512)[:, :, cb, :]
                    xts_done[rh] = pretr(
                        x_bf[rh*2048:(rh+1)*2048, cb*128:(cb+1)*128],
                        dst, x_cast[rh*8+7])
            emit_xts(0)

            WQSr = WQS.rearrange("p (jq f) -> p jq f", f=16384)
            wqs_done = None
            for cb in range(32):
                dst = WQSr.rearrange("p jq (cb f) -> p jq cb f", f=512)[:, :, cb, :]
                wqs_done = pretr(wq_bf[0:2048, cb*128:(cb+1)*128], dst,
                                 wq_cast[7])
            emit_xts(1)

            e.waitt("gpsimd", xts_done[1])
            wo_cast = []
            for i in range(16):
                wo_cast.append(e.dma("gpsimd", wo_bf[i*256:(i+1)*256, :],
                                     wo[i*256:(i+1)*256, :], "sGc"))
            g_cast = []
            for i in range(14):
                g_cast.append(e.dma("gpsimd", g_bf[i*512:(i+1)*512, :],
                                    gmat[i*512:(i+1)*512, :], "sGc"))
            u_cast = []
            for i in range(14):
                u_cast.append(e.dma("gpsimd", u_bf[i*512:(i+1)*512, :],
                                    umat[i*512:(i+1)*512, :], "sGc"))
            d_cast = []
            for i in range(16):
                d_cast.append(e.dma("gpsimd", d_bf[i*256:(i+1)*256, :],
                                    dmat[i*256:(i+1)*256, :], "sGc"))

            xtt_half_free = [0, 0]
            wqt_free = 0
            qbank_free = [0, 0, 0, 0]
            kvbank_done = [0, 0]
            vx_free = [0, 0]
            wo_strip_free = [0, 0]
            sweep_done_v = 0
            sweep_done_st = 0
            x1p_store = {}
            x1bf_stores = 0
            x1bf_pass_store = [0, 0]
            x1f_stores = 0
            smf_ve = [0] * 4
            smf_st = [0] * 4
            smf_ac = [0] * 4
            post_attn_ac = 0
            last_av_free = 0
            wos_done = None
            gus_done = None
            dts_done = []
            x1ts_done = [None, None]
            tb_free = [0, 0]

            e.wait("tensor", "sGc", kv_cast)
            e.wait("tensor", "sLd", c_consts)
            for i in range(2):
                src0 = XTT0 + i * 4096
                dst0 = WKT0 + i * 4096
                for cb in range(32):
                    b = cb % 2
                    pb0 = (6 + b) * 1024
                    e.wait("tensor", "sVe", tb_free[b])
                    e.tr(PSb[:, pb0:pb0+128], G[:, src0+cb*128:src0+(cb+1)*128],
                         eyeb[:, :], sem="sPe")
                    e.wait("vector", "sPe", e.c["sPe"])
                    tb_free[b] = e.vop("tensor_copy",
                                       G[:, dst0+cb*128:dst0+(cb+1)*128],
                                       PSb[:, pb0:pb0+128], sem="sVe")
            for cb in range(32):
                b = cb % 2
                pb0 = (6 + b) * 1024
                e.wait("tensor", "sVe", tb_free[b])
                e.tr(PSb[:, pb0:pb0+16],
                     G[0:16, XTT0+8192+cb*128:XTT0+8192+(cb+1)*128],
                     eyeb[0:16, 0:16], sem="sPe")
                e.wait("vector", "sPe", e.c["sPe"])
                tb_free[b] = e.vop("tensor_copy",
                                   gwt_sb[:, cb*16:(cb+1)*16],
                                   PSb[:, pb0:pb0+16], sem="sVe")
            wkvt_done_v = e.c["sVe"]
            wkvt_done_p = e.c["sPe"]

            for p in range(2):
                for jq in range(2):
                    e.waitt("sync", wqs_done)
                    e.wait("sync", "sPe", wqt_free)
                    e.dma("sync", G[:, WQT0:WQT0+16384],
                          WQS[:, (p*2+jq)*16384:(p*2+jq+1)*16384], "sLd")
                    wq_ready = e.c["sLd"]
                    for ss in range(8):
                        xb0 = XTT0
                        rh = ss // 4
                        e.waitt("sync", xts_done[rh])
                        xtt_ready = [0, 0]
                        for hf in range(2):
                            e.wait("sync", "sPe", xtt_half_free[hf])
                            e.dma("sync", G[:, xb0+hf*8192:xb0+(hf+1)*8192],
                                  XTS[:, ss*16384+hf*8192:ss*16384+(hf+1)*8192],
                                  "sLd")
                            xtt_ready[hf] = e.c["sLd"]
                        if p == 0 and jq == 0 and ss == 0:
                            e.wait("sync", "sPe", wkvt_done_p)

                        def xtt(cb):
                            return G[:, xb0+cb*512:xb0+(cb+1)*512]

                        kv = (p == 0 and jq == 0)
                        e.wait("tensor", "sLd", max(xtt_ready[0], wq_ready))
                        for sl in range(4):
                            e.wait("tensor", "sVe", qbank_free[sl])
                        if kv:
                            e.wait("tensor", "sVe", max(kvbank_done))
                        for cb in range(32):
                            if cb == 16:
                                e.wait("tensor", "sLd", xtt_ready[1])
                            if kv:
                                e.mm(PSf[:, 2048:2560],
                                     G[:, WKT0+cb*128:WKT0+(cb+1)*128],
                                     xtt(cb), cb == 0, cb == 31,
                                     sem="sPe" if cb == 31 else None)
                                e.mm(PSf[:, 2560:3072],
                                     G[:, WVT0+cb*128:WVT0+(cb+1)*128],
                                     xtt(cb), cb == 0, cb == 31,
                                     sem="sPe" if cb == 31 else None)
                            for sl in range(4):
                                e.mm(PSf[:, sl*512:(sl+1)*512],
                                     G[:, WQT0+cb*512+sl*128:WQT0+cb*512+(sl+1)*128],
                                     xtt(cb), cb == 0, cb == 31,
                                     sem="sPe" if sl == 3 else None)
                                if sl == 3 and (cb == 15 or cb == 31):
                                    xtt_half_free[cb // 16] = e.c["sPe"]
                        qmm_done = e.c["sPe"]
                        if kv:
                            e.wait("vector", "sPe", qmm_done)
                            e.vop("tensor_scalar",
                                  G[:, KT0+ss*512:KT0+(ss+1)*512],
                                  PSf[:, 2048:2560], bks[:, 0:1], None,
                                  ALU.add, sem="sVe")
                            kvbank_done[0] = e.c["sVe"]
                            vtb = lgb if ss % 2 == 0 else wfb
                            e.vop("tensor_scalar", vtb[:, 0:512],
                                  PSf[:, 2560:3072], bvs[:, 0:1], None,
                                  ALU.add, sem="sVe")
                            kvbank_done[1] = e.c["sVe"]
                            vt_evac = e.c["sVe"]
                            for st in range(4):
                                b = st % 2
                                pb0 = (6 + b) * 1024
                                kt = ss * 4 + st
                                e.wait("tensor", "sVe", vt_evac)
                                e.wait("tensor", "sVe", vx_free[b])
                                e.wait("tensor", "sVe", va_ones)
                                e.tr(PSb[:, pb0:pb0+128],
                                     vtb[:, st*128:(st+1)*128],
                                     eyeb[:, :], sem="sPe")
                                e.wait("vector", "sPe", e.c["sPe"])
                                vx_free[b] = e.vop(
                                    "tensor_copy",
                                    G[:, VA0+kt*129:VA0+kt*129+128],
                                    PSb[:, pb0:pb0+128], sem="sVe")
                        for sl in range(4):
                            slot = jq * 4 + sl
                            e.wait("vector", "sPe", qmm_done)
                            qbank_free[sl] = e.vop(
                                "tensor_scalar",
                                G[:, QT0+slot*4096+ss*512:QT0+slot*4096+(ss+1)*512],
                                PSf[:, sl*512:(sl+1)*512],
                                bqs[:, p*8+slot:p*8+slot+1], None,
                                ALU.add, sem="sVe")
                    if p == 0 and jq == 0:
                        WOSr = WOS.rearrange("p (jc f) -> p jc f", f=16384)
                        for cb2 in range(64):
                            qb, half = cb2 // 2, cb2 % 2
                            dst = WOSr[:, half*4:(half+1)*4, :].rearrange(
                                "p jc (qb f) -> p jc qb f", f=512)[:, :, qb, :]
                            wos_done = pretr(
                                wo_bf[half*2048:(half+1)*2048,
                                      qb*128:(qb+1)*128],
                                dst, wo_cast[15], store_q="scalar")
                    wqt_free = e.c["sPe"]
                qkv_done_v = e.c["sVe"]

                wo_strip_free[0] = max(wo_strip_free[0], wqt_free)
                wo_strip_free[1] = max(wo_strip_free[1], max(xtt_half_free))
                wot_ready = {}
                for jc in range(2):
                    wb0 = WQT0 if jc % 2 == 0 else XTT0
                    e.waitt("sync", wos_done)
                    e.wait("sync", "sPe", wo_strip_free[jc % 2])
                    e.dma("sync", G[:, wb0:wb0+16384],
                          WOS[:, jc*16384:(jc+1)*16384], "sLd")
                    wot_ready[jc] = e.c["sLd"]

                if stage == "qkv" and p == 0:
                    e.wait("sync", "sVe", qkv_done_v)
                    e.dma("sync", dbg["qt"], G[:, 0:32768], "sSt")
                    e.dma("sync", dbg["kt"], G[:, KT0:KT0+4096], "sSt")
                    e.dma("sync", dbg["va"], G[:, VA0:VA0+4128], "sSt")
                    e.wait("sync", "sSt", e.c["sSt"])
                    return

                exp_free = [0, 0]
                prb_free = [0, 0]
                av_free = 0
                sc_done = {}
                e.wait("tensor", "sVe", qkv_done_v)
                if p == 0:
                    e.wait("scalar", "sPe", wqt_free)
                else:
                    e.wait("scalar", "sVe", sweep_done_v)
                    e.wait("scalar", "sGc", sweep_done_st)

                def emit_scT(hh, qc, ktp):
                    buf = ktp % 2
                    sb0 = buf * 1024
                    q0 = QT0 + hh * 4096
                    e.wait("tensor", "sAc", exp_free[buf])
                    for half in range(2):
                        kt = ktp * 2 + half
                        e.mm(PSf[:, sb0+half*512:sb0+(half+1)*512],
                             G[:, KT0+kt*128:KT0+(kt+1)*128],
                             G[:, q0+qc*512:q0+(qc+1)*512],
                             True, True, sem="sPe" if half == 1 else None)
                    sc_done[(hh, qc, ktp)] = e.c["sPe"]

                triples = [(hh, qc, ktp) for hh in range(8)
                           for qc in range(8) for ktp in range(16)]
                for i, (hh, qc, ktp) in enumerate(triples):
                    buf = ktp % 2
                    sb0 = buf * 1024
                    q0 = QT0 + hh * 4096
                    if i == 0:
                        emit_scT(hh, qc, ktp)
                    e.wait("scalar", "sPe", sc_done[(hh, qc, ktp)])
                    e.wait("scalar", "sPe", prb_free[buf])
                    exp_free[buf] = e.act(
                        G[:, PRB0+sb0:PRB0+sb0+1024],
                        PSf[:, sb0:sb0+1024], AF.Exp,
                        bias=cst[:, 0:1], scale=SCALE, sem="sAc")
                    if i + 1 < len(triples):
                        emit_scT(*triples[i+1])
                    e.wait("tensor", "sAc", exp_free[buf])
                    if ktp == 0:
                        e.wait("tensor", "sVe", av_free)
                    for half in range(2):
                        kt = ktp * 2 + half
                        for qb in range(4):
                            a0 = 2048 + qb * 512
                            e.mm(PSf[:, a0:a0+129],
                                 G[:, PRB0+sb0+half*512+qb*128:
                                      PRB0+sb0+half*512+(qb+1)*128],
                                 G[:, VA0+kt*129:VA0+kt*129+129],
                                 kt == 0, kt == 31,
                                 sem="sPe" if (half == 1 and qb == 3) else None)
                    prb_free[buf] = e.c["sPe"]
                    if ktp == 15:
                        av_mm = e.c["sPe"]
                        e.wait("vector", "sPe", av_mm)
                        for qb in range(4):
                            a0 = 2048 + qb * 512
                            e.vop("reciprocal", st_f[:, 232+qb:233+qb],
                                  PSf[:, a0+128:a0+129],
                                  sem="sVe" if qb == 3 else None)
                        e.wait("vector", "sVe", e.c["sVe"])
                        for qb in range(4):
                            a0 = 2048 + qb * 512
                            qg = qc * 4 + qb
                            av_free = e.vop(
                                "tensor_scalar",
                                G[:, q0+qg*128:q0+(qg+1)*128],
                                PSf[:, a0:a0+128],
                                st_f[:, 232+qb:233+qb], None,
                                ALU.mult, sem="sVe")
                attn_done_v = e.c["sVe"]
                post_attn_ac = max(exp_free)
                last_av_free = av_free

                GUSgr = GUSg.rearrange("p (ic f) -> p ic f", f=16384)
                GUSur = GUSu.rearrange("p (ic f) -> p ic f", f=16384)
                X1Tr = X1TS.rearrange("p (th f) -> p th f", f=32768)

                def emit_gus(src_m, dstr, cw):
                    done = None
                    for ch in range(4):
                        r0 = ch * 2048
                        rr_n = min(2048, NI - r0)
                        nic = rr_n // 512
                        for cb in range(32):
                            dst = dstr[:, ch*4:ch*4+nic, :].rearrange(
                                "p ic (cb f) -> p ic cb f", f=512)[:, :, cb, :]
                            done = pretr(src_m[r0:r0+rr_n, cb*128:(cb+1)*128],
                                         dst, cw, store_q="gpsimd")
                    return done

                if p == 0:
                    gus_done_g = emit_gus(g_bf, GUSgr, g_cast[-1])
                else:
                    e.wait("sync", "sSt", smf_st[0])
                    e.wait("sync", "sSt", smf_st[1])
                    e.wait("sync", "sAc", smf_ac[0])
                    e.wait("sync", "sAc", smf_ac[1])
                    e.wait("sync", "sVe", smf_ve[0])
                    e.wait("sync", "sVe", smf_ve[1])
                    trb_state["bufs"] = bounce4
                    DTSr = DTS.rearrange("p (q f) -> p q f", f=56 * 1024)
                    for half in range(2):
                        for ib in range(56):
                            dst = DTSr.rearrange(
                                "p q (ib f) -> p q ib f", f=1024)[:, :, ib, :]
                            dts_done.append(pretr(
                                d_bf[half*2048:(half+1)*2048,
                                     ib*128:(ib+1)*128],
                                dst[:, half*2:half*2+2, :],
                                d_cast[7 if half == 0 else 15],
                                store_q="gpsimd"))
                    gus_done = emit_gus(u_bf, GUSur, u_cast[-1])
                    x1src = x1bf_pass_store[0]
                    for cb in range(32):
                        dst = X1Tr.rearrange(
                            "p th (cb f) -> p th cb f", f=1024)[:, 0, cb, :]
                        x1ts_done[0] = pretr(
                            x1bf_d[0:1024, cb*128:(cb+1)*128],
                            dst, x1src, store_q="gpsimd")

                if stage == "attn" and p == 0:
                    e.wait("sync", "sVe", attn_done_v)
                    e.dma("sync", dbg["qt"], G[:, 0:32768], "sSt")
                    e.wait("sync", "sSt", e.c["sSt"])
                    return

                wo_psum_free = [attn_done_v, attn_done_v]
                e.wait("tensor", "sAc", post_attn_ac)
                nrm_done = [0, 0]
                prb_f0 = WKT0 // 2
                x1bf_tile0 = WVT0
                wo_pe_start = e.c["sPe"]
                for bk in ("s0", "s1"):
                    bs = trb_state["free"].get(bk)
                    if bs is not None:
                        e.waitt("sync", bs)
                        e.waitt("vector", bs)
                        e.waitt("scalar", bs)
                trb_state["bufs"] = bounce2
                for jc in range(8):
                    wbuf = jc % 2
                    wb0 = WQT0 if wbuf == 0 else XTT0
                    for hh in range(8):
                        bank = hh % 2
                        q0 = QT0 + hh * 4096
                        e.wait("tensor", "sLd", wot_ready[jc])
                        e.wait("tensor", "sVe", wo_psum_free[bank])
                        for qb in range(32):
                            e.mm(PSf[:, bank*512:(bank+1)*512],
                                 G[:, q0+qb*128:q0+(qb+1)*128],
                                 G[:, wb0+qb*512:wb0+(qb+1)*512],
                                 qb == 0, qb == 31,
                                 sem="sPe" if qb == 31 else None)
                        womm = e.c["sPe"]
                        if hh == 7:
                            wo_strip_free[wbuf] = womm
                        ti = (jc * 8 + hh) % 4
                        rl0 = (p * 8 + hh) * 128
                        e.wait("scalar", "sAc", smf_ac[ti])
                        e.wait("scalar", "sVe", smf_ve[ti])
                        e.dma("scalar", sm2_f[:, ti*512:(ti+1)*512],
                              xr_b[rl0:rl0+128, jc*512:(jc+1)*512], "sSc")
                        xr_ready = e.c["sSc"]
                        e.wait("vector", "sPe", womm)
                        e.wait("vector", "sSc", xr_ready)
                        e.wait("vector", "sSt", smf_st[ti])
                        e.wait("vector", "sAc", smf_ac[ti])
                        e.vop("tensor_tensor", sm_f[:, ti*512:(ti+1)*512],
                              PSf[:, bank*512:(bank+1)*512],
                              sm2_f[:, ti*512:(ti+1)*512], ALU.add, sem="sVe")
                        wo_psum_free[bank] = e.c["sVe"]
                        e.wait("vector", "sVe", e.c["sVe"])
                        e.vop("tensor_tensor", sm_f[:, ti*512:(ti+1)*512],
                              sm_f[:, ti*512:(ti+1)*512],
                              bos[:, jc*512:(jc+1)*512], ALU.add, sem="sVe")
                        tmp_v = e.c["sVe"]
                        e.wait("vector", "sVe", tmp_v)
                        smf_ve[ti] = e.vop(
                            "reduce_sum", st_f[:, hh*8+jc:hh*8+jc+1],
                            sm_f[:, ti*512:(ti+1)*512], axis=AX.X, sem="sVe")
                        e.wait("scalar", "sVe", tmp_v)
                        smf_ac[ti] = e.act(
                            sm2_f[:, ti*512:(ti+1)*512],
                            sm_f[:, ti*512:(ti+1)*512], AF.Square,
                            accum=st_f[:, 64+hh*8+jc:64+hh*8+jc+1], sem="sAc")
                        e.wait("sync", "sVe", tmp_v)
                        smf_st[ti] = e.dma(
                            "sync", x1pre[rl0:rl0+128, jc*512:(jc+1)*512],
                            sm_f[:, ti*512:(ti+1)*512], "sSt")
                        x1p_store[(p, hh, jc)] = smf_st[ti]

                        if jc == 7:
                            stats_v = e.c["sVe"]
                            stats_a = e.c["sAc"]
                            e.wait("vector", "sVe", stats_v)
                            e.wait("vector", "sAc", stats_a)
                            e.vop("reduce_sum", st_f[:, 128+hh:129+hh],
                                  st_f[:, hh*8:hh*8+8], axis=AX.X, sem="sVe")
                            e.vop("reduce_sum", st_f[:, 136+hh:137+hh],
                                  st_f[:, 64+hh*8:64+hh*8+8], axis=AX.X,
                                  sem="sVe")
                            e.wait("vector", "sVe", e.c["sVe"])
                            e.vop("tensor_scalar", st_f[:, 144+hh:145+hh],
                                  st_f[:, 128+hh:129+hh], 1.0/4096.0, None,
                                  ALU.mult, sem="sVe")
                            e.wait("vector", "sVe", e.c["sVe"])
                            e.vop("tensor_tensor", st_f[:, 152+hh:153+hh],
                                  st_f[:, 144+hh:145+hh],
                                  st_f[:, 144+hh:145+hh], ALU.mult, sem="sVe")
                            e.wait("vector", "sVe", e.c["sVe"])
                            e.vop("scalar_tensor_tensor",
                                  st_f[:, 160+hh:161+hh],
                                  st_f[:, 136+hh:137+hh], 1.0/4096.0,
                                  st_f[:, 152+hh:153+hh], ALU.mult,
                                  ALU.subtract, sem="sVe")
                            var_v = e.c["sVe"]
                            e.wait("scalar", "sVe", var_v)
                            e.act(st_f[:, 168+hh:169+hh],
                                  st_f[:, 160+hh:161+hh], AF.Ln,
                                  bias=cst[:, 2:3], scale=1.0, sem="sAc")
                            e.wait("scalar", "sAc", e.c["sAc"])
                            e.act(st_f[:, 176+hh:177+hh],
                                  st_f[:, 168+hh:169+hh], AF.Exp,
                                  bias=cst[:, 5:6], scale=-0.5, sem="sAc")
                            rstd_a = e.c["sAc"]
                            for jc2 in range(8):
                                t2 = jc2 % 2
                                if jc2 == 0:
                                    e.wait("vector", "sGc", x1bf_stores)
                                e.wait("gpsimd", "sSt", x1p_store[(p, hh, jc2)])
                                e.wait("gpsimd", "sVe", nrm_done[t2])
                                if p == 0 and hh == 0 and jc2 == 0:
                                    e.wait("gpsimd", "sPe", wo_pe_start)
                                e.wait("gpsimd", "sGc", x1f_stores)
                                e.dma("gpsimd",
                                      Gf[:, prb_f0+t2*512:prb_f0+(t2+1)*512],
                                      x1pre[rl0:rl0+128, jc2*512:(jc2+1)*512],
                                      "sGc")
                                ch_ready = e.c["sGc"]
                                e.wait("vector", "sGc", ch_ready)
                                e.wait("vector", "sAc", rstd_a)
                                e.vop("tensor_scalar",
                                      G[:, x1bf_tile0+jc2*512:x1bf_tile0+(jc2+1)*512],
                                      Gf[:, prb_f0+t2*512:prb_f0+(t2+1)*512],
                                      st_f[:, 144+hh:145+hh],
                                      st_f[:, 176+hh:177+hh],
                                      ALU.subtract, ALU.mult, sem="sVe")
                                nrm_done[t2] = e.vop(
                                    "tensor_scalar",
                                    Gf[:, prb_f0+t2*512:prb_f0+(t2+1)*512],
                                    Gf[:, prb_f0+t2*512:prb_f0+(t2+1)*512],
                                    st_f[:, 144+hh:145+hh],
                                    st_f[:, 176+hh:177+hh],
                                    ALU.subtract, ALU.mult, sem="sVe")
                                e.wait("gpsimd", "sVe", nrm_done[t2])
                                x1f_stores = e.dma(
                                    "gpsimd",
                                    x1f_d[rl0:rl0+128, jc2*512:(jc2+1)*512],
                                    Gf[:, prb_f0+t2*512:prb_f0+(t2+1)*512],
                                    "sGc")
                            bf_v = e.c["sVe"]
                            e.wait("gpsimd", "sVe", bf_v)
                            x1bf_stores = e.dma(
                                "gpsimd", x1bf_d[rl0:rl0+128, :],
                                G[:, x1bf_tile0:x1bf_tile0+4096], "sGc")
                    if jc + 2 < 8:
                        e.wait("sync", "sPe", wo_strip_free[wbuf])
                        e.dma("sync", G[:, wb0:wb0+16384],
                              WOS[:, (jc+2)*16384:(jc+3)*16384], "sLd")
                        wot_ready[jc + 2] = e.c["sLd"]
                sweep_done_v = e.c["sVe"]
                sweep_done_st = e.c["sGc"]
                x1bf_pass_store[p] = e.c["sGc"]
                qbank_free[0] = max(qbank_free[0], wo_psum_free[0])
                qbank_free[1] = max(qbank_free[1], wo_psum_free[1])
                wqt_free = max(wqt_free, wo_strip_free[0])
                xtt_half_free[0] = max(xtt_half_free[0], wo_strip_free[1])
                xtt_half_free[1] = max(xtt_half_free[1], wo_strip_free[1])

                if p == 1:
                    x1src_ready = e.c["sGc"]
                    for cb in range(32):
                        dst = X1Tr.rearrange(
                            "p th (cb f) -> p th cb f", f=1024)[:, 1, cb, :]
                        x1ts_done[1] = pretr(
                            x1bf_d[1024:2048, cb*128:(cb+1)*128],
                            dst, x1src_ready, store_q="gpsimd")

            if stage == "wo":
                e.wait("gpsimd", "sGc", sweep_done_st)
                for i in range(8):
                    e.dma("gpsimd", dbg["x1f"][i*256:(i+1)*256, :],
                          x1f_d[i*256:(i+1)*256, :], "sGc")
                e.wait("sync", "sGc", e.c["sGc"])
                return

            ic_segs = []
            for ic in range(14):
                segs = []
                i0 = ic * 512
                while i0 < (ic + 1) * 512:
                    ex = i0 // INTER
                    i1 = min((ex + 1) * INTER, (ic + 1) * 512)
                    segs.append((i0 - ic * 512, i1 - ic * 512, ex))
                    i0 = i1
                ic_segs.append(segs)

            gu_grp_free = [0] * 4
            x1t_free = 0
            hw_ic_last = {}
            gp_free = [sweep_done_v, sweep_done_v]
            silu_free = [0, 0]
            silu_rd = [0, 0]
            hw_st = [0] * 4
            hw_ve = [0] * 4
            for th in range(2):
                e.waitt("sync", x1ts_done[th])
                e.wait("sync", "sPe", x1t_free)
                e.dma("sync", G[:, 0:32768],
                      X1TS[:, th*32768:(th+1)*32768], "sLd")
                x1t_ready = e.c["sLd"]

                e.wait("tensor", "sLd", x1t_ready)
                if th == 0:
                    e.wait("tensor", "sGc", sweep_done_st)
                    e.wait("tensor", "sAc", post_attn_ac)
                    e.wait("tensor", "sVe", last_av_free)
                for cb in range(32):
                    e.mm(PSf[0:16, 3072:3584],
                         gwt_sb[:, cb*16:(cb+1)*16],
                         G[:, cb*1024:cb*1024+512],
                         cb == 0, cb == 31, sem=None)
                    e.mm(PSf[0:16, 3584:4096],
                         gwt_sb[:, cb*16:(cb+1)*16],
                         G[:, cb*1024+512:(cb+1)*1024],
                         cb == 0, cb == 31,
                         sem="sPe" if cb == 31 else None)
                lgT_p = e.c["sPe"]
                e.wait("vector", "sPe", lgT_p)
                e.vop("tensor_copy", sm2_f[0:16, 0:1024],
                      PSf[0:16, 3072:4096], sem="sVe")
                lgT_v = e.c["sVe"]
                for tb in range(8):
                    b = tb % 2
                    pf0 = (6 + b) * 512
                    e.wait("tensor", "sVe", max(lgT_v, tb_free[b]))
                    e.mm(PSf[:, pf0:pf0+16],
                         sm2_f[0:16, tb*128:(tb+1)*128], eyef[0:16, :],
                         True, True, sem="sPe")
                    e.wait("vector", "sPe", e.c["sPe"])
                    tb_free[b] = e.vop(
                        "tensor_copy", lg_f[:, (th*8+tb)*16:(th*8+tb+1)*16],
                        PSf[:, pf0:pf0+16], sem="sVe")
                for tb in range(8):
                    tt = th * 8 + tb
                    L = lg_f[:, tt*16:(tt+1)*16]
                    W = w_f[:, tt*16:(tt+1)*16]
                    e.wait("vector", "sVe", e.c["sVe"])
                    e.vop("reduce_max", tk_f[:, 0:1], L, axis=AX.X, sem="sVe")
                    e.wait("vector", "sVe", e.c["sVe"])
                    e.vop("tensor_scalar", tk_f[:, 1:2], tk_f[:, 0:1],
                          -1.0, None, ALU.mult, sem="sVe")
                    e.vop("tensor_scalar", tk_f[:, 16:32], L,
                          tk_f[:, 0:1], None, ALU.is_equal, sem="sVe")
                    e.wait("vector", "sVe", e.c["sVe"])
                    e.vop("scalar_tensor_tensor", tk_f[:, 32:48],
                          tk_f[:, 16:32], -1e30, L, ALU.mult, ALU.add,
                          sem="sVe")
                    e.wait("vector", "sVe", e.c["sVe"])
                    e.vop("reduce_max", tk_f[:, 2:3], tk_f[:, 32:48],
                          axis=AX.X, sem="sVe")
                    e.wait("vector", "sVe", e.c["sVe"])
                    e.vop("tensor_scalar", tk_f[:, 16:32], L,
                          tk_f[:, 2:3], None, ALU.is_ge, sem="sVe")
                    ge_v = e.c["sVe"]
                    e.wait("scalar", "sVe", ge_v)
                    e.act(tk_f[:, 48:64], L, AF.Exp,
                          bias=tk_f[:, 1:2], scale=1.0, sem="sAc")
                    e.wait("vector", "sAc", e.c["sAc"])
                    e.vop("tensor_tensor", tk_f[:, 32:48],
                          tk_f[:, 48:64], tk_f[:, 16:32], ALU.mult, sem="sVe")
                    e.wait("vector", "sVe", e.c["sVe"])
                    e.vop("reduce_sum", tk_f[:, 3:4], tk_f[:, 32:48],
                          axis=AX.X, sem="sVe")
                    e.wait("vector", "sVe", e.c["sVe"])
                    e.vop("reciprocal", tk_f[:, 4:5], tk_f[:, 3:4], sem="sVe")
                    e.wait("vector", "sVe", e.c["sVe"])
                    e.vop("tensor_scalar", W, tk_f[:, 32:48],
                          tk_f[:, 4:5], None, ALU.mult, sem="sVe")
                rt_v = e.c["sVe"]

                for ic in range(14):
                    for grp in range(4):
                        e.waitt("sync", gus_done_g)
                        e.waitt("sync", gus_done)
                        e.wait("sync", "sPe", gu_grp_free[grp])
                        e.dma("sync",
                              G[:, WQT0+grp*4096:WQT0+(grp+1)*4096],
                              GUSg[:, ic*16384+grp*4096:ic*16384+(grp+1)*4096],
                              "sLd")
                        e.dma("sync",
                              G[:, XTT0+grp*4096:XTT0+(grp+1)*4096],
                              GUSu[:, ic*16384+grp*4096:ic*16384+(grp+1)*4096],
                              "sLd")
                    gu_ready = e.c["sLd"]
                    e.wait("tensor", "sLd", gu_ready)
                    for tt8 in range(8):
                        tt = th * 8 + tt8
                        pb = tt8 % 2
                        gp0 = pb * 1024
                        up0 = pb * 1024 + 512
                        e.wait("tensor", "sVe", gp_free[pb])
                        e.wait("tensor", "sAc", silu_free[pb])
                        for cb in range(32):
                            e.mm(PSf[:, gp0:gp0+512],
                                 G[:, cb*1024+tt8*128:cb*1024+(tt8+1)*128],
                                 G[:, WQT0+cb*512:WQT0+(cb+1)*512],
                                 cb == 0, cb == 31, sem=None)
                            e.mm(PSf[:, up0:up0+512],
                                 G[:, cb*1024+tt8*128:cb*1024+(tt8+1)*128],
                                 G[:, XTT0+cb*512:XTT0+(cb+1)*512],
                                 cb == 0, cb == 31,
                                 sem="sPe" if cb == 31 else None)
                            if tt8 == 7 and cb % 8 == 7:
                                gu_grp_free[cb // 8] = e.c["sPe"]
                        gud = e.c["sPe"]
                        hbuf = (ic * 8 + tt8) % 4
                        e.wait("scalar", "sPe", gud)
                        e.wait("scalar", "sVe", silu_rd[pb])
                        silu_free[pb] = e.act(
                            sm2b[:, pb*512+2048:pb*512+2560],
                            PSf[:, gp0:gp0+512], AF.Silu, sem="sAc")
                        e.wait("vector", "sAc", silu_free[pb])
                        e.wait("vector", "sSt", hw_st[hbuf])
                        e.wait("vector", "sPe", gud)
                        for (o0, o1, ex) in ic_segs[ic]:
                            e.vop("scalar_tensor_tensor",
                                  smb[:, hbuf*1024+o0:hbuf*1024+o1],
                                  sm2b[:, pb*512+2048+o0:pb*512+2048+o1],
                                  w_f[:, tt*16+ex:tt*16+ex+1],
                                  PSf[:, up0+o0:up0+o1],
                                  ALU.mult, ALU.mult, sem="sVe")
                        gp_free[pb] = e.c["sVe"]
                        silu_rd[pb] = e.c["sVe"]
                        hw_ve[hbuf] = e.c["sVe"]
                        e.wait("sync", "sVe", hw_ve[hbuf])
                        hw_st[hbuf] = e.dma(
                            "sync", hw_d[tt*128:(tt+1)*128, ic*512:(ic+1)*512],
                            smb[:, hbuf*1024:hbuf*1024+512], "sSt")
                        if th == 1 and tt8 == 7:
                            hw_ic_last[ic] = hw_st[hbuf]
                x1t_free = e.c["sPe"]
            hw_stores = e.c["sSt"]

            hws_done = []
            for ib in range(56):
                hws_done.append(pretr(hw_d[0:2048, ib*128:(ib+1)*128],
                                      HWS[:, ib*2048:(ib+1)*2048],
                                      ("sSt", hw_ic_last[ib // 4]),
                                      store_q="gpsimd"))

            if stage == "ffn1":
                e.dma("sync", dbg["w"], w_f[:, :], "sSt")
                e.wait("gpsimd", "sSt", hw_stores)
                for i in range(8):
                    e.dma("gpsimd", dbg["hw"][i*256:(i+1)*256, :],
                          hw_d[i*256:(i+1)*256, :], "sGc")
                for i in range(8):
                    e.dma("gpsimd", dbg["x1f"][i*256:(i+1)*256, :],
                          x1f_d[i*256:(i+1)*256, :], "sGc")
                e.wait("sync", "sGc", e.c["sGc"])
                e.wait("sync", "sSt", e.c["sSt"])
                return

            SUP = 8
            NSUP = 7
            f1_v = e.c["sVe"]
            f1_a = e.c["sAc"]
            e.wait("tensor", "sVe", f1_v)
            e.wait("tensor", "sAc", f1_a)
            cacc_done = [0, 0]
            set_free = [0, 0]
            cp_st = 0
            for q in range(4):
                c0 = q * 1024
                for sp in range(NSUP):
                    ib0 = sp * SUP
                    sset = sp % 2
                    s0 = 32768 + sset * 24576
                    e.waitt("sync", dts_done[-1])
                    e.waitt("sync", hws_done[ib0 + SUP - 1])
                    e.wait("sync", "sPe", set_free[sset])
                    e.dma("sync", G[:, s0:s0+8192],
                          DTS[:, q*57344+ib0*1024:q*57344+(ib0+SUP)*1024],
                          "sLd")
                    e.dma("sync", G[:, s0+8192:s0+24576],
                          HWS[:, ib0*2048:(ib0+SUP)*2048], "sLd")
                    strips_ready = e.c["sLd"]
                    e.wait("tensor", "sLd", strips_ready)
                    for tt in range(16):
                        pg = tt % 2
                        p0 = pg * 1024
                        e.wait("tensor", "sVe", cacc_done[pg])
                        for k in range(SUP):
                            for nq in range(2):
                                e.mm(PSf[:, p0+nq*512:p0+(nq+1)*512],
                                     G[:, s0+8192+k*2048+tt*128:
                                          s0+8192+k*2048+(tt+1)*128],
                                     G[:, s0+k*1024+nq*512:s0+k*1024+(nq+1)*512],
                                     k == 0, k == SUP - 1,
                                     sem="sPe" if (k == SUP - 1 and nq == 1)
                                     else None)
                        dmm = e.c["sPe"]
                        if tt == 15:
                            set_free[sset] = dmm
                        e.wait("vector", "sPe", dmm)
                        if q == 0 and sp == 0:
                            e.wait("vector", "sPe", x1t_free)
                        if sp == 0:
                            e.wait("vector", "sSt", cp_st)
                            cacc_done[pg] = e.vop(
                                "tensor_copy", Gf[:, tt*1024:(tt+1)*1024],
                                PSf[:, p0:p0+1024], sem="sVe")
                        else:
                            cacc_done[pg] = e.vop(
                                "tensor_tensor",
                                Gf[:, tt*1024:(tt+1)*1024],
                                Gf[:, tt*1024:(tt+1)*1024],
                                PSf[:, p0:p0+1024], ALU.add, sem="sVe")
                if q < 3:
                    e.wait("sync", "sVe", max(cacc_done))
                    for tt in range(16):
                        e.dma("sync", cp0[tt*128:(tt+1)*128, q*1024:(q+1)*1024],
                              Gf[:, tt*1024:(tt+1)*1024], "sSt")
                    cp_st = e.c["sSt"]

            X1R = 16384
            CPT = 20480
            X2A = 23552
            SQD = 27648
            fin_v = 0
            fin_st = 0
            f2_pe = e.c["sPe"]
            for tt in range(16):
                r0 = tt * 128
                if tt == 0:
                    e.wait("sync", "sPe", f2_pe)
                e.wait("sync", "sVe", fin_v)
                e.wait("sync", "sSt", fin_st)
                e.dma("sync", Gf[:, X1R:X1R+4096], x1f_d[r0:r0+128, :], "sLd")
                e.dma("sync", Gf[:, CPT:CPT+3072], cp0[r0:r0+128, :], "sLd")
                fr = e.c["sLd"]
                e.wait("vector", "sLd", fr)
                e.wait("vector", "sVe", fin_v)
                e.vop("tensor_tensor", Gf[:, X2A:X2A+3072],
                      Gf[:, CPT:CPT+3072], Gf[:, X1R:X1R+3072], ALU.add,
                      sem="sVe")
                e.vop("tensor_tensor", Gf[:, X2A+3072:X2A+4096],
                      Gf[:, tt*1024:(tt+1)*1024], Gf[:, X1R+3072:X1R+4096],
                      ALU.add, sem="sVe")
                x2p_v = e.c["sVe"]
                e.wait("vector", "sVe", x2p_v)
                e.vop("reduce_sum", tk_f[:, 8:9], Gf[:, X2A:X2A+4096],
                      axis=AX.X, sem="sVe")
                s1v = e.c["sVe"]
                e.wait("scalar", "sVe", x2p_v)
                e.act(Gf[:, SQD:SQD+4096], Gf[:, X2A:X2A+4096], AF.Square,
                      accum=tk_f[:, 10:11], sem="sAc")
                s2a = e.c["sAc"]
                e.wait("vector", "sVe", s1v)
                e.wait("vector", "sAc", s2a)
                e.vop("tensor_scalar", tk_f[:, 14:15], tk_f[:, 8:9],
                      1.0/4096.0, None, ALU.mult, sem="sVe")
                e.wait("vector", "sVe", e.c["sVe"])
                e.vop("tensor_tensor", tk_f[:, 15:16], tk_f[:, 14:15],
                      tk_f[:, 14:15], ALU.mult, sem="sVe")
                e.wait("vector", "sVe", e.c["sVe"])
                e.vop("scalar_tensor_tensor", tk_f[:, 20:21],
                      tk_f[:, 10:11], 1.0/4096.0, tk_f[:, 15:16],
                      ALU.mult, ALU.subtract, sem="sVe")
                varv = e.c["sVe"]
                e.wait("scalar", "sVe", varv)
                e.act(tk_f[:, 21:22], tk_f[:, 20:21], AF.Ln,
                      bias=cst[:, 2:3], scale=1.0, sem="sAc")
                e.wait("scalar", "sAc", e.c["sAc"])
                e.act(tk_f[:, 22:23], tk_f[:, 21:22], AF.Exp,
                      bias=cst[:, 5:6], scale=-0.5, sem="sAc")
                rsta = e.c["sAc"]
                e.wait("vector", "sAc", rsta)
                e.vop("tensor_scalar", Gf[:, X2A:X2A+4096],
                      Gf[:, X2A:X2A+4096], tk_f[:, 14:15],
                      tk_f[:, 22:23], ALU.subtract, ALU.mult, sem="sVe")
                fin_v = e.c["sVe"]
                e.wait("sync", "sVe", fin_v)
                fin_st = e.dma("sync", out[r0:r0+128, :],
                               Gf[:, X2A:X2A+4096], "sSt")
            e.wait("sync", "sSt", e.c["sSt"])

        block.sync(lambda eng: emit("sync", eng))
        block.gpsimd(lambda eng: emit("gpsimd", eng))
        block.tensor(lambda eng: emit("tensor", eng))
        block.vector(lambda eng: emit("vector", eng))
        block.scalar(lambda eng: emit("scalar", eng))
    return nc


def _mk_inputs(inputs):
    x = np.ascontiguousarray(np.asarray(inputs["L0_attn_hidden_state"]))
    Wq = np.asarray(inputs["L0_attn_Wq"]); bq = np.asarray(inputs["L0_attn_bq"])
    Wk = np.asarray(inputs["L0_attn_Wk"]); bk = np.asarray(inputs["L0_attn_bk"])
    Wv = np.asarray(inputs["L0_attn_Wv"]); bv = np.asarray(inputs["L0_attn_bv"])
    Wo = np.asarray(inputs["L0_attn_Wo"]); bo = np.asarray(inputs["L0_attn_bo"])
    consts = np.zeros((128, 8), dtype=np.float32)
    consts[:, 0] = EXP_SHIFT
    consts[:, 2] = EPS
    eye_bf = np.eye(128, dtype=ml_dtypes.bfloat16)
    eye_f = np.zeros((128, 16), dtype=np.float32)
    eye_f[:16, :16] = np.eye(16, dtype=np.float32)
    bo_bc = np.ascontiguousarray(
        np.broadcast_to(bo.astype(np.float32)[None, :], (128, CD)))
    gm = np.ascontiguousarray(np.asarray(inputs["L0_ffn_g"]))
    um = np.ascontiguousarray(np.asarray(inputs["L0_ffn_u"]))
    dm = np.ascontiguousarray(np.asarray(inputs["L0_ffn_d"]))
    gwm = np.ascontiguousarray(np.asarray(inputs["L0_ffn_gate_w"]))
    Wk = np.ascontiguousarray(Wk); Wv = np.ascontiguousarray(Wv)
    Wo = np.ascontiguousarray(Wo)
    maps = []
    for b in range(4):
        for g in range(2):
            maps.append(dict(
                x_b=x[b],
                xr_b=np.ascontiguousarray(x[b, 2048*g:2048*(g+1), :]),
                wq=np.ascontiguousarray(Wq[2048*g:2048*(g+1)]),
                bq2=np.ascontiguousarray(
                    bq[2048*g:2048*(g+1)].reshape(16, 128).T),
                wk=Wk, bk2=np.ascontiguousarray(bk.reshape(128, 1)),
                wv=Wv, bv2=np.ascontiguousarray(bv.reshape(128, 1)),
                wo=Wo, bo_bc=bo_bc, gwi=gwm,
                gmat=gm, umat=um, dmat=dm,
                consts=consts, eye_bf=eye_bf, eye_f=eye_f,
            ))
    return maps


def kernel(**inputs):
    maps = _mk_inputs(inputs)
    nc = build_nc()
    res = run_bass_kernel_spmd(nc, maps, core_ids=list(range(8)))
    full = np.empty((4, S, CD), dtype=np.float32)
    for b in range(4):
        for g in range(2):
            full[b, 2048*g:2048*(g+1), :] = res.results[b*2+g]["out"]
    return full
